# version 27
# baseline (speedup 1.0000x reference)
import contextlib
import numpy as np

import concourse.bass as bass
import concourse.mybir as mybir
from concourse.bass_utils import run_bass_kernel_spmd

F32 = mybir.dt.float32
BF16 = mybir.dt.bfloat16
AF = mybir.ActivationFunctionType
ALU = mybir.AluOpType
AXX = mybir.AxisListType.X

P = 128
NDB = 8
DM = 1024
LSEQ = 2048
NB = 2
NH = 2
LP = 1024
LC = 512
NCH = 2
II = 2816
NJ = 11
NIC = 2
NE = 8
NST = 16
RDT = 64
EPS = 1e-5
AEPS = 1e-8


class Ctr:
    __slots__ = ("h", "n")

    def __init__(self, h):
        self.h = h
        self.n = 0


def build(cfg=None):
    cfg = dict(cfg or {})
    npass = cfg.get("npass", NB * NH)
    do_ffn1 = cfg.get("ffn1", True)
    do_mamba = cfg.get("mamba", True)
    do_ffn2 = cfg.get("ffn2", True)
    n_exp = cfg.get("ne", NE)
    split2 = cfg.get("split2", False)
    fence = cfg.get("fence", False)

    tap = cfg.get("tap")
    nc = bass.Bass("TRN2", target_bir_lowering=False)

    x_h = nc.dram_tensor("x", [NB, DM, LSEQ], F32, kind="ExternalInput").ap()
    wxT_h = nc.dram_tensor("wxT", [DM, DM], F32, kind="ExternalInput").ap()
    wzT_h = nc.dram_tensor("wzT", [DM, DM], F32, kind="ExternalInput").ap()
    woT_h = nc.dram_tensor("woT", [DM, DM], F32, kind="ExternalInput").ap()
    wpT_h = nc.dram_tensor("wpT", [DM, 96], F32, kind="ExternalInput").ap()
    wdtT_h = nc.dram_tensor("wdtT", [RDT, DM], F32, kind="ExternalInput").ap()
    bdt_h = nc.dram_tensor("bdt", [P, NDB], F32, kind="ExternalInput").ap()
    alog_h = nc.dram_tensor("alog", [P, NDB, NST], F32, kind="ExternalInput").ap()
    dp_h = nc.dram_tensor("dp", [P, NDB], F32, kind="ExternalInput").ap()
    convw_h = nc.dram_tensor("convw", [P, NDB, 4], F32, kind="ExternalInput").ap()
    gT_h = nc.dram_tensor("gT", [NE, DM, II], F32, kind="ExternalInput").ap()
    uT_h = nc.dram_tensor("uT", [NE, DM, II], F32, kind="ExternalInput").ap()
    dT_h = nc.dram_tensor("dT", [NE, II, DM], F32, kind="ExternalInput").ap()
    wrT_h = nc.dram_tensor("wrT", [DM, 7], F32, kind="ExternalInput").ap()
    rb_h = nc.dram_tensor("rb", [7, 1], F32, kind="ExternalInput").ap()
    out_h = nc.dram_tensor("out", [NB, DM, LSEQ], F32, kind="ExternalOutput").ap()
    scrg_h = nc.dram_tensor("scrg", [NE, DM, II], BF16, kind="Internal").ap()
    scru_h = nc.dram_tensor("scru", [NE, DM, II], BF16, kind="Internal").ap()
    scrd_h = nc.dram_tensor("scrd", [NE, II, DM], BF16, kind="Internal").ap()
    tap_h = None
    if tap:
        tap_h = nc.dram_tensor("tap", [P, NDB, LP], F32,
                               kind="ExternalOutput").ap()

    es = contextlib.ExitStack()
    with es:
        ec = es.enter_context

        xr = ec(nc.sbuf_tensor("xr", [P, NDB, LP], F32))
        xn = ec(nc.sbuf_tensor("xn", [P, NDB, LP], BF16))
        identf = ec(nc.sbuf_tensor("identf", [P, P], F32))
        pick16 = ec(nc.sbuf_tensor("pick16", [16, 16, P], BF16))
        pickf = ec(nc.sbuf_tensor("pickf", [16, 7, P], F32))
        ones_col = ec(nc.sbuf_tensor("ones_col", [P, 1], F32))
        ones_bf = ec(nc.sbuf_tensor("ones_bf", [P, 1], BF16))
        ones_row = ec(nc.sbuf_tensor("ones_row", [1, P], F32))
        bdt_sb = ec(nc.sbuf_tensor("bdt_sb", [P, NDB], F32))
        dp_sb = ec(nc.sbuf_tensor("dp_sb", [P, NDB], F32))
        convw_sb = ec(nc.sbuf_tensor("convw_sb", [P, NDB, 4], F32))
        alog_sb = ec(nc.sbuf_tensor("alog_sb", [P, NDB, NST], F32))
        A_sb = ec(nc.sbuf_tensor("A_sb", [P, NDB, NST], F32))
        invA_sb = ec(nc.sbuf_tensor("invA_sb", [P, NDB, NST], F32))
        ninvA_sb = ec(nc.sbuf_tensor("ninvA_sb", [P, NDB, NST], F32))
        wrT_sb = ec(nc.sbuf_tensor("wrT_sb", [P, NDB, 7], BF16))
        rb_sb = ec(nc.sbuf_tensor("rb_sb", [7, 1], F32))
        wdtT_sb = ec(nc.sbuf_tensor("wdtT_sb", [RDT, DM], BF16))
        wpT_sb = ec(nc.sbuf_tensor("wpT_sb", [P, NDB, 96], BF16))
        h_state = ec(nc.sbuf_tensor("h_state", [P, NDB, NST], F32))
        pad_save = ec(nc.sbuf_tensor("pad_save", [P, NDB, 3], BF16))
        cst = ec(nc.sbuf_tensor("cst", [P, 4], F32))
        _cvals = [EPS, 1.0, -1.0, -0.5]
        for _i, _v in enumerate(_cvals):
            nc.const_aps.register(F32, _v, cst[:, _i:_i + 1])

        s_x = Ctr(ec(nc.semaphore("s_x")))
        s_w = Ctr(ec(nc.semaphore("s_w")))
        s_ps = Ctr(ec(nc.semaphore("s_ps")))
        s_in = Ctr(ec(nc.semaphore("s_in")))
        s_ev = Ctr(ec(nc.semaphore("s_ev")))
        s_dn = Ctr(ec(nc.semaphore("s_dn")))
        s_rs = Ctr(ec(nc.semaphore("s_rs")))
        s_p2 = Ctr(ec(nc.semaphore("s_p2")))
        s_d2 = Ctr(ec(nc.semaphore("s_d2")))
        s_e2 = Ctr(ec(nc.semaphore("s_e2")))
        s_pl = Ctr(ec(nc.semaphore("s_pl")))

        with nc.Block() as blk:
            @blk.sync
            def _(sync):
                sync.dma_start(out=bdt_sb[:], in_=bdt_h).then_inc(s_x.h, 16)
                sync.dma_start(out=dp_sb[:], in_=dp_h).then_inc(s_x.h, 16)
                sync.dma_start(out=convw_sb[:], in_=convw_h).then_inc(s_x.h, 16)
                sync.dma_start(out=alog_sb[:], in_=alog_h).then_inc(s_x.h, 16)
                sync.dma_start(out=rb_sb[:], in_=rb_h).then_inc(s_x.h, 16)
            s_x.n += 80

            @blk.gpsimd
            def _(gp):
                gp.dma_start(out=wrT_sb[:],
                             in_=wrT_h.rearrange("(db p) e -> p db e", p=P)
                             ).then_inc(s_w.h, 16)
                gp.dma_start(out=wdtT_sb[:], in_=wdtT_h).then_inc(s_w.h, 16)
                gp.dma_start(out=wpT_sb[:],
                             in_=wpT_h.rearrange("(db p) m -> p db m", p=P)
                             ).then_inc(s_w.h, 16)
            s_w.n += 48

        with nc.Block() as blk, \
                nc.sbuf_tensor("iota_a", [P, P], F32) as iota_a, \
                nc.sbuf_tensor("iota_b", [P, P], F32) as iota_b, \
                nc.sbuf_tensor("piota_a", [16, 16, P], F32) as piota_a, \
                nc.sbuf_tensor("piota_b", [16, 16, P], F32) as piota_b:
            @blk.gpsimd
            def _(gp):
                gp.iota(iota_a[:], [[0, P]], channel_multiplier=1,
                        allow_small_or_imprecise_dtypes=True)
                gp.iota(iota_b[:], [[1, P]], channel_multiplier=0,
                        allow_small_or_imprecise_dtypes=True)
                gp.iota(piota_a[:], [[0, 16], [0, P]], channel_multiplier=1,
                        allow_small_or_imprecise_dtypes=True)
                gp.iota(piota_b[:], [[1, 16], [0, P]], channel_multiplier=0,
                        allow_small_or_imprecise_dtypes=True
                        ).then_inc(s_pl.h, 1)
            s_pl.n += 1

            @blk.vector
            def _(v):
                v.wait_ge(s_pl.h, s_pl.n)
                v.tensor_tensor(identf[:], iota_a[:], iota_b[:], ALU.is_equal)
                v.tensor_tensor(pick16[:], piota_a[:], piota_b[:], ALU.is_equal)
                v.tensor_tensor(pickf[:], piota_a[:, 0:7, :],
                                piota_b[:, 0:7, :], ALU.is_equal)
                v.memset(ones_col[:], 1.0)
                v.memset(ones_bf[:], 1.0)
                v.memset(ones_row[:], 1.0)
                v.memset(h_state[:], 0.0)
                v.memset(pad_save[:], 0.0)
                for _i, _v in enumerate(_cvals):
                    v.memset(cst[:, _i:_i + 1], _v)
                v.drain()

            @blk.scalar
            def _(s):
                s.wait_ge(s_x.h, s_x.n)
                s.activation(A_sb[:], alog_sb[:], AF.Exp).then_inc(s_ev.h, 1)
            s_ev.n += 1

        with nc.Block() as blk:
            for _reg in (blk.tensor, blk.vector, blk.scalar, blk.gpsimd,
                         blk.sync):
                def _body(eng, _x=s_x.n, _w=s_w.n):
                    eng.wait_ge(s_x.h, _x)
                    eng.wait_ge(s_w.h, _w)
                _reg(_body)

        with nc.Block() as blk:
            @blk.vector
            def _(v):
                v.wait_ge(s_ev.h, s_ev.n)
                v.tensor_single_scalar(A_sb[:], A_sb[:], -1.0, ALU.mult)
                v.drain()
                v.tensor_single_scalar(invA_sb[:], A_sb[:], AEPS, ALU.add)
                v.drain()
                v.reciprocal(invA_sb[:], invA_sb[:])
                v.drain()
                v.tensor_single_scalar(ninvA_sb[:], invA_sb[:], -1.0, ALU.mult)
                v.drain()

        def emit_ln(tag, xin_dma=None, tok0=0, ntok=LP):
            NLCW = ntok // 512
            with nc.Block() as blk, contextlib.ExitStack() as sc:
                pc = sc.enter_context
                ps_s1 = pc(nc.psum_tensor("ps_s1_%s" % tag, [1, ntok], F32))
                ps_s2 = pc(nc.psum_tensor("ps_s2_%s" % tag, [1, ntok], F32))
                ps_b1 = pc(nc.psum_tensor("ps_b1_%s" % tag, [P, 2, 512], F32))
                xsq = pc(nc.sbuf_tensor("xsq_%s" % tag, [P, 2, 512], BF16))
                stat = pc(nc.sbuf_tensor("stat_%s" % tag, [1, 4, ntok], F32))
                tmpf = pc(nc.sbuf_tensor("tmpf_%s" % tag, [P, 512], F32))

                bev, bp2, bd2, be2 = s_ev.n, s_p2.n, s_d2.n, s_e2.n
                NSQ = NLCW * NDB

                def W(lc):
                    return slice(tok0 + lc * 512, tok0 + (lc + 1) * 512)

                if xin_dma is not None:
                    @blk.sync
                    def _(sync):
                        b, hh = xin_dma
                        srcx = x_h[b].rearrange("(db p) l -> p db l", p=P)[
                            :, :, hh * LP:(hh + 1) * LP]
                        sync.dma_start(out=xr[:], in_=srcx).then_inc(s_x.h, 16)
                    s_x.n += 16

                @blk.scalar
                def _(s):
                    s.wait_ge(s_x.h, s_x.n)
                    for db in range(NDB):
                        s.activation(xn[:, db, tok0:tok0 + ntok],
                                     xr[:, db, tok0:tok0 + ntok], AF.Identity)
                    s.drain().then_inc(s_e2.h, 1)
                    k = 0
                    for lc in range(NLCW):
                        for db in range(NDB):
                            if k >= 2:
                                s.wait_ge(s_p2.h, bp2 + k - 1)
                            s.activation(xsq[:, k % 2, :], xr[:, db, W(lc)],
                                         AF.Square).then_inc(s_ev.h, 1)
                            k += 1
                    s.wait_ge(s_d2.h, bd2 + 1)
                    s.activation(stat[:, 1, :], stat[:, 1, :], AF.Ln, bias=EPS)
                    s.drain()
                    s.activation(stat[:, 2, :], stat[:, 1, :], AF.Exp,
                                 scale=-0.5)
                    s.drain().then_inc(s_e2.h, 1)
                s_ev.n = bev + NSQ
                s_e2.n = be2 + 2

                @blk.tensor
                def _(t):
                    t.wait_ge(s_e2.h, be2 + 1)
                    for lc in range(NLCW):
                        for db in range(NDB):
                            t.matmul(ps_s1[0:1, lc * 512:(lc + 1) * 512],
                                     ones_bf[:], xn[:, db, W(lc)],
                                     start=(db == 0), stop=(db == NDB - 1))
                    k = 0
                    for lc in range(NLCW):
                        for db in range(NDB):
                            t.wait_ge(s_ev.h, bev + k + 1)
                            t.matmul(ps_s2[0:1, lc * 512:(lc + 1) * 512],
                                     ones_bf[:], xsq[:, k % 2, :],
                                     start=(db == 0), stop=(db == NDB - 1)
                                     ).then_inc(s_p2.h, 1)
                            k += 1
                    t.wait_ge(s_d2.h, bd2 + 2)
                    for lc in range(NLCW):
                        if lc >= 1:
                            t.wait_ge(s_d2.h, bd2 + 2 + lc)
                        t.matmul(ps_b1[:, 0, :], ones_row[:],
                                 stat[0:1, 2, lc * 512:(lc + 1) * 512],
                                 start=True, stop=True)
                        t.matmul(ps_b1[:, 1, :], ones_row[:],
                                 stat[0:1, 3, lc * 512:(lc + 1) * 512],
                                 start=True, stop=True).then_inc(s_p2.h, 1)
                s_p2.n = bp2 + NSQ + NLCW

                @blk.vector
                def _(v):
                    v.wait_ge(s_p2.h, bp2 + NSQ)
                    v.tensor_single_scalar(stat[:, 0, :], ps_s1[0:1, :],
                                           1.0 / DM, ALU.mult)
                    v.drain()
                    v.tensor_tensor(stat[:, 1, :], stat[:, 0, :],
                                    stat[:, 0, :], ALU.mult)
                    v.drain()
                    v.scalar_tensor_tensor(stat[:, 1, :], ps_s2[0:1, :],
                                           1.0 / DM, stat[:, 1, :], ALU.mult,
                                           ALU.subtract)
                    v.drain().then_inc(s_d2.h, 1)
                    v.wait_ge(s_e2.h, be2 + 2)
                    v.scalar_tensor_tensor(stat[:, 3, :], stat[:, 0, :], -1.0,
                                           stat[:, 2, :], ALU.mult, ALU.mult)
                    v.drain().then_inc(s_d2.h, 1)
                    for lc in range(NLCW):
                        v.wait_ge(s_p2.h, bp2 + NSQ + lc + 1)
                        op = None
                        for db in range(NDB):
                            if fence:
                                v.drain()
                            v.tensor_tensor(tmpf[:], xr[:, db, W(lc)],
                                            ps_b1[:, 0, :], ALU.mult)
                            if fence:
                                v.drain()
                            op = v.tensor_tensor(xn[:, db, W(lc)],
                                                 tmpf[:], ps_b1[:, 1, :],
                                                 ALU.add)
                        op.then_inc(s_d2.h, 1)
                s_d2.n = bd2 + 2 + NLCW

        def emit_router(tag, wbs, tok0=0, ntok=LP):
            NLCW = ntok // 512
            NTT = ntok // P
            with contextlib.ExitStack() as sc:
                pc = sc.enter_context
                ps_r = pc(nc.psum_tensor("ps_r_%s" % tag, [7, 2, 512], F32))
                lt_ps = pc(nc.psum_tensor("lt_ps_%s" % tag, [P, NTT, 7], F32))
                wcm_ps = pc(nc.psum_tensor("wcm_ps_%s" % tag, [7, ntok], F32))
                wb_ps = pc(nc.psum_tensor("wb_ps_%s" % tag, [P, 2, 512], F32))
                lg_cm = pc(nc.sbuf_tensor("lg_cm_%s" % tag, [7, ntok], F32))
                lg_tm = pc(nc.sbuf_tensor("lg_tm_%s" % tag, [P, NTT, 7], F32))
                e_tm = pc(nc.sbuf_tensor("e_tm_%s" % tag, [P, NTT, 7], F32))
                w_tm = pc(nc.sbuf_tensor("w_tm_%s" % tag, [P, NTT, 7], F32))
                mstat = pc(nc.sbuf_tensor("mstat_%s" % tag, [P, 5, NTT], F32))
                mk = pc(nc.sbuf_tensor("mk_%s" % tag, [P, NTT, 7], F32))
                em = pc(nc.sbuf_tensor("em_%s" % tag, [P, NTT, 7], F32))
                w_cm = pc(nc.sbuf_tensor("w_cm_%s" % tag, [7, ntok], F32))

                bev, bp2, bd2, be2 = s_ev.n, s_p2.n, s_d2.n, s_e2.n

                def W(lc):
                    return slice(tok0 + lc * 512, tok0 + (lc + 1) * 512)

                with nc.Block() as blk:
                    @blk.tensor
                    def _(t):
                        for lc in range(NLCW):
                            for db in range(NDB):
                                t.matmul(ps_r[:, lc % 2, :], wrT_sb[:, db, :],
                                         xn[:, db, W(lc)],
                                         start=(db == 0), stop=(db == NDB - 1)
                                         ).then_inc(
                                             s_p2.h, 1 if db == NDB - 1 else 0)
                        t.wait_ge(s_ev.h, bev + NLCW)
                        for tt in range(NTT):
                            t.transpose(lt_ps[:, tt, :],
                                        lg_cm[0:7, tt * P:(tt + 1) * P],
                                        identf[0:7, 0:7]).then_inc(
                                            s_p2.h, 1 if tt == NTT - 1 else 0)
                        t.wait_ge(s_d2.h, bd2 + 2)
                        for tt in range(NTT):
                            t.transpose(wcm_ps[:, tt * P:(tt + 1) * P],
                                        w_tm[:, tt, :], identf[:]).then_inc(
                                            s_p2.h, 1 if tt == NTT - 1 else 0)
                        t.wait_ge(s_e2.h, be2 + 3)
                        k = 0
                        for e in range(7):
                            for lc in range(NLCW):
                                if k >= 2:
                                    t.wait_ge(s_ev.h, bev + NLCW + k - 1)
                                t.matmul(wb_ps[:, k % 2, :], pickf[0:7, e, :],
                                         w_cm[:, lc * 512:(lc + 1) * 512],
                                         start=True, stop=True
                                         ).then_inc(s_p2.h, 1)
                                k += 1
                    s_p2.n = bp2 + NLCW + 2 + 7 * NLCW

                    @blk.scalar
                    def _(s):
                        for lc in range(NLCW):
                            s.wait_ge(s_p2.h, bp2 + lc + 1)
                            s.activation(lg_cm[0:7, lc * 512:(lc + 1) * 512],
                                         ps_r[:, lc % 2, :], AF.Identity,
                                         bias=rb_sb[:, 0:1]).then_inc(s_ev.h, 1)
                        s.wait_ge(s_p2.h, bp2 + NLCW + 1)
                        s.activation(lg_tm[:], lt_ps[:], AF.Identity
                                     ).then_inc(s_e2.h, 1)
                        s.wait_ge(s_d2.h, bd2 + 1)
                        op = None
                        for tt in range(NTT):
                            op = s.activation(e_tm[:, tt, :], lg_tm[:, tt, :],
                                              AF.Exp,
                                              bias=mstat[:, 1, tt:tt + 1])
                        op.then_inc(s_e2.h, 1)
                        s.wait_ge(s_p2.h, bp2 + NLCW + 2)
                        s.activation(w_cm[:], wcm_ps[0:7, :], AF.Identity
                                     ).then_inc(s_e2.h, 1)
                        k = 0
                        for e in range(7):
                            for lc in range(NLCW):
                                s.wait_ge(s_p2.h, bp2 + NLCW + 2 + k + 1)
                                s.activation(wbs[:, e, W(lc)],
                                             wb_ps[:, k % 2, :], AF.Identity
                                             ).then_inc(s_ev.h, 1)
                                k += 1
                    s_ev.n = bev + NLCW + 7 * NLCW
                    s_e2.n = be2 + 3

                    @blk.vector
                    def _(v):
                        v.wait_ge(s_e2.h, be2 + 1)
                        for tt in range(NTT):
                            v.tensor_reduce(mstat[:, 0, tt:tt + 1],
                                            lg_tm[:, tt, :], AXX, ALU.max)
                        v.drain()
                        v.tensor_single_scalar(mstat[:, 1, :], mstat[:, 0, :],
                                               -1.0, ALU.mult)
                        v.drain().then_inc(s_d2.h, 1)
                        v.wait_ge(s_e2.h, be2 + 2)
                        for tt in range(NTT):
                            v.tensor_reduce(mstat[:, 2, tt:tt + 1],
                                            e_tm[:, tt, :], AXX, ALU.max)
                        v.drain()
                        for tt in range(NTT):
                            v.tensor_scalar(mk[:, tt, :], e_tm[:, tt, :],
                                            mstat[:, 2, tt:tt + 1], None,
                                            ALU.is_lt)
                        v.drain()
                        for tt in range(NTT):
                            v.tensor_tensor(em[:, tt, :], e_tm[:, tt, :],
                                            mk[:, tt, :], ALU.mult)
                        v.drain()
                        for tt in range(NTT):
                            v.tensor_reduce(mstat[:, 3, tt:tt + 1],
                                            em[:, tt, :], AXX, ALU.max)
                        v.drain()
                        v.tensor_tensor(mstat[:, 4, :], mstat[:, 2, :],
                                        mstat[:, 3, :], ALU.add)
                        v.drain()
                        v.reciprocal(mstat[:, 4, :], mstat[:, 4, :])
                        v.drain()
                        for tt in range(NTT):
                            v.tensor_scalar(mk[:, tt, :], e_tm[:, tt, :],
                                            mstat[:, 3, tt:tt + 1], None,
                                            ALU.is_ge)
                        v.drain()
                        for tt in range(NTT):
                            v.tensor_tensor(em[:, tt, :], e_tm[:, tt, :],
                                            mk[:, tt, :], ALU.mult)
                        v.drain()
                        for tt in range(NTT):
                            v.tensor_scalar(w_tm[:, tt, :], em[:, tt, :],
                                            mstat[:, 4, tt:tt + 1], 0.5,
                                            ALU.mult, ALU.mult)
                        v.drain().then_inc(s_d2.h, 1)
                    s_d2.n = bd2 + 2

        def emit_ffn(tag, wbs, sg, su, sd, inter, skip_first_load=False,
                     first_call=False, plan=((0, 11), (11, 11)),
                     tok0=0, ntok=LP):
            with nc.Block() as blk, contextlib.ExitStack() as sc:
                pc = sc.enter_context
                ps_g = pc(nc.psum_tensor("ps_g_%s" % tag, [P, 2, 512], F32))
                ps_u = pc(nc.psum_tensor("ps_u_%s" % tag, [P, 2, 512], F32))
                ps_o = pc(nc.psum_tensor("ps_o_%s" % tag, [P, 2, LP], F32))
                sgact = pc(nc.sbuf_tensor("sgact_%s" % tag, [P, 2, 512], F32))
                tmp = pc(nc.sbuf_tensor("tmpr_%s" % tag, [P, 2, LP], F32))

                bw, bps, bin_, bev, bdn, brs = (s_w.n, s_ps.n, s_in.n,
                                                s_ev.n, s_dn.n, s_rs.n)
                if skip_first_load:
                    bw = s_w.n - 48
                NICP = len(plan)
                NIT = n_exp * NICP
                NLCW = ntok // 512
                jbase = [0]
                for _, jc in plan:
                    jbase.append(jbase[-1] + jc)
                npair = [NLCW * b for b in jbase]
                PPE = npair[-1]

                def WL(lc):
                    return slice(tok0 + lc * 512, tok0 + (lc + 1) * 512)

                def pair0(it):
                    e, ic = divmod(it, NICP)
                    return e * PPE + npair[ic]

                def dcnt(it):
                    return it * NDB

                bx0 = s_x.n

                @blk.gpsimd
                def _(gp):
                    gsrc, usrc, dsrc = ((gT_h, uT_h, dT_h) if first_call else
                                        (scrg_h, scru_h, scrd_h))
                    for it in range(1 if skip_first_load else 0, NIT):
                        e, ic = divmod(it, NICP)
                        js, jc = plan[ic]
                        gsl = gsrc[e].rearrange("(db p) i -> p db i", p=P)[
                            :, :, js * P:(js + jc) * P]
                        usl = usrc[e].rearrange("(db p) i -> p db i", p=P)[
                            :, :, js * P:(js + jc) * P]
                        dsl = dsrc[e].rearrange("(jj p) dd -> p jj dd",
                                                p=P)[:, js:js + jc, :]
                        if it > 0:
                            gp.wait_ge(s_ps.h, bps + pair0(it))
                            if first_call:
                                gp.wait_ge(s_x.h, bx0 + it * 48)
                        gp.dma_start(out=sg[:, :, 0:jc * P], in_=gsl
                                     ).then_inc(s_w.h, 16)
                        gp.dma_start(out=su[:, :, 0:jc * P], in_=usl
                                     ).then_inc(s_w.h, 16)
                        if it > 0:
                            gp.wait_ge(s_dn.h, bdn + dcnt(it))
                            if first_call:
                                gp.wait_ge(s_x.h, bx0 + it * 48)
                        gp.dma_start(out=sd[:, 0:jc, :], in_=dsl
                                     ).then_inc(s_w.h, 16)
                s_w.n = bw + NIT * 48

                if first_call:
                    @blk.sync
                    def _(sync):
                        for it in range(NIT):
                            e, ic = divmod(it, NICP)
                            js, jc = plan[ic]
                            gds = scrg_h[e].rearrange("(db p) i -> p db i", p=P)[
                                :, :, js * P:(js + jc) * P]
                            uds = scru_h[e].rearrange("(db p) i -> p db i", p=P)[
                                :, :, js * P:(js + jc) * P]
                            dds = scrd_h[e].rearrange(
                                "(jj p) dd -> p jj dd", p=P)[:, js:js + jc, :]
                            gu_rdy = 48 if it == 0 else 32
                            sync.wait_ge(s_w.h, bw + it * 48 + gu_rdy)
                            sync.dma_start(out=gds, in_=sg[:, :, 0:jc * P]
                                           ).then_inc(s_x.h, 16)
                            sync.dma_start(out=uds, in_=su[:, :, 0:jc * P]
                                           ).then_inc(s_x.h, 16)
                            sync.wait_ge(s_w.h, bw + it * 48 + 48)
                            sync.dma_start(out=dds, in_=sd[:, 0:jc, :]
                                           ).then_inc(s_x.h, 16)
                        sync.wait_ge(s_x.h, bx0 + NIT * 48)
                    s_x.n = bx0 + NIT * 48

                @blk.tensor
                def _(t):
                    for it in range(NIT):
                        e, ic = divmod(it, NICP)
                        js, jc = plan[ic]
                        gu_ready = 48 if (it == 0 and skip_first_load) else 32
                        t.wait_ge(s_w.h, bw + it * 48 + gu_ready)
                        for j in range(jc):
                            for lc in range(NLCW):
                                n = pair0(it) + j * NLCW + lc
                                sl = n % 2
                                if n >= 2:
                                    t.wait_ge(s_in.h, bin_ + n - 1)
                                for k in range(NDB):
                                    t.matmul(ps_g[:, sl, :],
                                             sg[:, k, j * P:(j + 1) * P],
                                             xn[:, k, WL(lc)],
                                             start=(k == 0),
                                             stop=(k == NDB - 1))
                                mm = None
                                for k in range(NDB):
                                    mm = t.matmul(
                                        ps_u[:, sl, :],
                                        su[:, k, j * P:(j + 1) * P],
                                        xn[:, k, WL(lc)],
                                        start=(k == 0), stop=(k == NDB - 1))
                                mm.then_inc(s_ps.h, 1)
                        t.wait_ge(s_w.h, bw + it * 48 + 48)
                        for db in range(NDB):
                            gdb = it * NDB + db
                            sl = db % 2
                            if gdb >= 2:
                                t.wait_ge(s_rs.h, brs + gdb - 1)
                            mm = None
                            for j in range(jc):
                                if db == 0:
                                    t.wait_ge(s_in.h,
                                              bin_ + pair0(it) + (j + 1) * NLCW)
                                for lc in range(NLCW):
                                    mm = t.matmul(
                                        ps_o[:, sl, lc * 512:(lc + 1) * 512],
                                        sd[:, j, db * P:(db + 1) * P],
                                        inter[:, j, lc * 512:(lc + 1) * 512],
                                        start=(j == 0), stop=(j == jc - 1))
                            mm.then_inc(s_dn.h, 1)
                s_ps.n = bps + n_exp * PPE
                s_dn.n = bdn + NIT * NDB

                @blk.scalar
                def _(s):
                    for it in range(NIT):
                        e, ic = divmod(it, NICP)
                        js, jc = plan[ic]
                        for j in range(jc):
                            for lc in range(NLCW):
                                n = pair0(it) + j * NLCW + lc
                                sl = n % 2
                                s.wait_ge(s_ps.h, bps + n + 1)
                                if n >= 2:
                                    s.wait_ge(s_in.h, bin_ + n - 1)
                                s.activation(sgact[:, sl, :], ps_g[:, sl, :],
                                             AF.Silu).then_inc(s_ev.h, 1)
                s_ev.n = bev + n_exp * PPE

                @blk.vector
                def _(v):
                    for it in range(NIT):
                        e, ic = divmod(it, NICP)
                        js, jc = plan[ic]
                        for j in range(jc):
                            for lc in range(NLCW):
                                n = pair0(it) + j * NLCW + lc
                                sl = n % 2
                                v.wait_ge(s_ev.h, bev + n + 1)
                                v.tensor_tensor(
                                    inter[:, j, lc * 512:(lc + 1) * 512],
                                    sgact[:, sl, :], ps_u[:, sl, :],
                                    ALU.mult).then_inc(s_in.h, 1)
                        for db in range(NDB):
                            gdb = it * NDB + db
                            sl = db % 2
                            v.wait_ge(s_dn.h, bdn + gdb + 1)
                            xw = xr[:, db, tok0:tok0 + ntok]
                            if e == 0:
                                op = v.scalar_tensor_tensor(
                                    xw, ps_o[:, sl, 0:ntok], 0.5,
                                    xw, ALU.mult, ALU.add)
                            else:
                                if fence:
                                    v.drain()
                                v.scalar_tensor_tensor(
                                    tmp[:, sl, 0:ntok], ps_o[:, sl, 0:ntok],
                                    1.0, wbs[:, e - 1, tok0:tok0 + ntok],
                                    ALU.bypass, ALU.mult)
                                if fence:
                                    v.drain()
                                op = v.tensor_tensor(xw, xw,
                                                     tmp[:, sl, 0:ntok],
                                                     ALU.add)
                            op.then_inc(s_rs.h, 1)
                s_in.n = bin_ + n_exp * PPE
                s_rs.n = brs + NIT * NDB

        def emit_mamba(pp, hh):
            with contextlib.ExitStack() as mw:
                mc = mw.enter_context
                W1 = mc(nc.sbuf_tensor("W1_%d" % pp, [P, NDB, DM], BF16))
                Wz_sb = mc(nc.sbuf_tensor("Wzs_%d" % pp, [P, NDB, DM], BF16))
                Wo_sb = mc(nc.sbuf_tensor("Wos_%d" % pp, [P, NDB, DM], BF16))
                xproj = mc(nc.sbuf_tensor("xpj_%d" % pp, [P, NDB, LC + 3], BF16))
                dt_sb = mc(nc.sbuf_tensor("dt_%d" % pp, [P, NDB, LC], F32))
                u_sb = mc(nc.sbuf_tensor("u_%d" % pp, [P, NDB, LC], F32))
                zs_sb = mc(nc.sbuf_tensor("zs_%d" % pp, [P, NDB, LC], BF16))
                y_sb = mc(nc.sbuf_tensor("y_%d" % pp, [P, NDB, LC], F32))
                dtun_bf = mc(nc.sbuf_tensor("dtun_%d" % pp, [RDT, LC], BF16))
                B_bf = mc(nc.sbuf_tensor("Bbf_%d" % pp, [16, LC], BF16))
                C_bf = mc(nc.sbuf_tensor("Cbf_%d" % pp, [16, LC], BF16))
                Bb_sb = mc(nc.sbuf_tensor("Bb_%d" % pp, [P, 2, LC], F32))
                Cb_sb = mc(nc.sbuf_tensor("Cb_%d" % pp, [P, 2, LC], F32))
                a_bufs = mc(nc.sbuf_tensor("abuf_%d" % pp, [P, 2, LC], F32))
                am1_bufs = mc(nc.sbuf_tensor("am1_%d" % pp, [P, 2, LC], F32))
                w2_bufs = mc(nc.sbuf_tensor("w2b_%d" % pp, [P, 2, LC], F32))
                h_bufs = mc(nc.sbuf_tensor("hbuf_%d" % pp, [P, 2, LC], F32))
                ch_bufs = mc(nc.sbuf_tensor("chb_%d" % pp, [P, 1, LC], F32))
                tdt = mc(nc.sbuf_tensor("tdt_%d" % pp, [P, 1, LC], F32))

                with nc.Block() as blk:
                    bw = s_w.n

                    @blk.gpsimd
                    def _(gp):
                        gp.dma_start(
                            out=W1[:],
                            in_=wxT_h.rearrange("(kb p) m -> p kb m", p=P)
                        ).then_inc(s_w.h, 16)
                        gp.dma_start(
                            out=Wz_sb[:],
                            in_=wzT_h.rearrange("(kb p) m -> p kb m", p=P)
                        ).then_inc(s_w.h, 16)
                        gp.dma_start(
                            out=Wo_sb[:],
                            in_=woT_h.rearrange("(kb p) m -> p kb m", p=P)
                        ).then_inc(s_w.h, 16)
                    s_w.n = bw + 48

                bufs = (W1, Wz_sb, Wo_sb, xproj, dt_sb, u_sb, zs_sb, y_sb,
                        dtun_bf, B_bf, C_bf, Bb_sb, Cb_sb, a_bufs, am1_bufs,
                        w2_bufs, h_bufs, ch_bufs, tdt)
                for c in range(NCH):
                    with nc.named_scope("p%d_mc%d" % (pp, c)):
                        emit_mamba_chunk(pp, hh, c, bufs)

                if hh == 0:
                    with nc.Block() as blk:
                        @blk.vector
                        def _(v):
                            v.tensor_copy(pad_save[:], xproj[:, :, LC:LC + 3])
                            v.drain()

        def emit_mamba_chunk(pp, hh, c, bufs):
            (W1, Wz_sb, Wo_sb, xproj, dt_sb, u_sb, zs_sb, y_sb, dtun_bf,
             B_bf, C_bf, Bb_sb, Cb_sb, a_bufs, am1_bufs, w2_bufs, h_bufs,
             ch_bufs, tdt) = bufs
            t0 = c * LC

            with nc.named_scope("mA_%d_%d" % (pp, c)), \
                    nc.Block() as blk, contextlib.ExitStack() as sc:
                pc = sc.enter_context
                ps_m = pc(nc.psum_tensor("psA_%d_%d" % (pp, c), [P, 2, 512], F32))
                ps_p1 = pc(nc.psum_tensor("psP1_%d_%d" % (pp, c), [80, LC], F32))
                ps_p2 = pc(nc.psum_tensor("psP2_%d_%d" % (pp, c), [16, LC], F32))

                bw, bp2, bev, bd2, be2 = (s_w.n, s_p2.n, s_ev.n, s_d2.n,
                                          s_e2.n)

                @blk.tensor
                def _(t):
                    t.wait_ge(s_w.h, bw)
                    for db in range(NDB):
                        sl = db % 2
                        if db >= 2:
                            t.wait_ge(s_ev.h, bev + db - 1)
                        for k in range(NDB):
                            t.matmul(ps_m[:, sl, 0:LC],
                                     W1[:, k, db * P:(db + 1) * P],
                                     xn[:, k, t0:t0 + LC],
                                     start=(k == 0), stop=(k == NDB - 1)
                                     ).then_inc(s_p2.h,
                                                1 if k == NDB - 1 else 0)
                    for k in range(NDB):
                        t.matmul(ps_p1[:], wpT_sb[:, k, 0:80],
                                 xn[:, k, t0:t0 + LC],
                                 start=(k == 0), stop=(k == NDB - 1))
                    mm = None
                    for k in range(NDB):
                        mm = t.matmul(ps_p2[:], wpT_sb[:, k, 80:96],
                                      xn[:, k, t0:t0 + LC],
                                      start=(k == 0), stop=(k == NDB - 1))
                    mm.then_inc(s_p2.h, 1)
                    t.wait_ge(s_ev.h, bev + NDB + 1)
                    for db in range(NDB):
                        sl = db % 2
                        if db >= 2:
                            t.wait_ge(s_e2.h, be2 + db - 1)
                        t.matmul(ps_m[:, sl, 0:LC],
                                 wdtT_sb[:, db * P:(db + 1) * P],
                                 dtun_bf[:], start=True, stop=True
                                 ).then_inc(s_p2.h, 1)
                    for db in range(NDB):
                        sl = db % 2
                        if db < 2:
                            t.wait_ge(s_e2.h, be2 + 7 + db)
                        else:
                            t.wait_ge(s_e2.h, be2 + NDB + db - 1)
                        for k in range(NDB):
                            t.matmul(ps_m[:, sl, 0:LC],
                                     Wz_sb[:, k, db * P:(db + 1) * P],
                                     xn[:, k, t0:t0 + LC],
                                     start=(k == 0), stop=(k == NDB - 1)
                                     ).then_inc(s_p2.h,
                                                1 if k == NDB - 1 else 0)
                s_p2.n = bp2 + NDB + 1 + NDB + NDB

                @blk.scalar
                def _(s):
                    s.wait_ge(s_d2.h, bd2 + 1)
                    for db in range(NDB):
                        sl = db % 2
                        s.wait_ge(s_p2.h, bp2 + db + 1)
                        s.activation(xproj[:, db, 3:3 + LC], ps_m[:, sl, 0:LC],
                                     AF.Identity).then_inc(s_ev.h, 1)
                    s.wait_ge(s_p2.h, bp2 + NDB + 1)
                    s.activation(dtun_bf[:], ps_p1[0:64, :], AF.Identity)
                    s.activation(B_bf[:], ps_p1[64:80, :], AF.Identity)
                    s.activation(C_bf[:], ps_p2[:], AF.Identity)
                    s.drain().then_inc(s_ev.h, 1)
                    for db in range(NDB):
                        sl = db % 2
                        lnb = (ch_bufs[:, 0, :] if db % 2 == 0
                               else w2_bufs[:, 0, :])
                        s.wait_ge(s_p2.h, bp2 + NDB + 1 + db + 1)
                        if db >= 2:
                            s.wait_ge(s_d2.h, bd2 + db)
                        s.activation(tdt[:, 0, :], ps_m[:, sl, 0:LC], AF.Identity,
                                     bias=bdt_sb[:, db:db + 1])
                        if fence:
                            s.drain()
                        s.activation(lnb, tdt[:, 0, :], AF.Abs)
                        if fence:
                            s.drain()
                        s.activation(lnb, lnb, AF.Exp, scale=-1.0)
                        if fence:
                            s.drain()
                        s.activation(lnb, lnb, AF.Ln, bias=1.0)
                        if fence:
                            s.drain()
                        s.activation(dt_sb[:, db, :], tdt[:, 0, :],
                                     AF.Relu).then_inc(s_e2.h, 1)
                    for db in range(NDB):
                        sl = db % 2
                        s.wait_ge(s_p2.h, bp2 + 2 * NDB + 1 + db + 1)
                        s.activation(zs_sb[:, db, :], ps_m[:, sl, 0:LC],
                                     AF.Silu).then_inc(s_e2.h, 1)
                    s.wait_ge(s_d2.h, bd2 + NDB + 2)
                    op = None
                    for db in range(NDB):
                        if fence:
                            s.drain()
                        op = s.activation(u_sb[:, db, :], u_sb[:, db, :],
                                          AF.Silu)
                    op.then_inc(s_e2.h, 1)
                s_ev.n = bev + NDB + 1
                s_e2.n = be2 + NDB + NDB + 1

                @blk.vector
                def _(v):
                    if c == 0 and hh == 0:
                        v.memset(xproj[:, :, 0:3], 0.0)
                    elif c == 0:
                        v.tensor_copy(xproj[:, :, 0:3], pad_save[:])
                    else:
                        v.tensor_copy(xproj[:, :, 0:3], xproj[:, :, LC:LC + 3])
                    v.drain().then_inc(s_d2.h, 1)
                    for db in range(NDB):
                        lnb = (ch_bufs[:, 0, :] if db % 2 == 0
                               else w2_bufs[:, 0, :])
                        v.wait_ge(s_e2.h, be2 + db + 1)
                        v.tensor_tensor(dt_sb[:, db, :], dt_sb[:, db, :],
                                        lnb, ALU.add).then_inc(s_d2.h, 1)
                    v.wait_ge(s_ev.h, bev + NDB)
                    op = None
                    for db in range(NDB):
                        v.tensor_scalar(u_sb[:, db, :], xproj[:, db, 0:LC],
                                        convw_sb[:, db, 0:1], None, ALU.mult)
                        for k in range(1, 4):
                            if fence:
                                v.drain()
                            op = v.scalar_tensor_tensor(
                                u_sb[:, db, :], xproj[:, db, k:k + LC],
                                convw_sb[:, db, k:k + 1], u_sb[:, db, :],
                                ALU.mult, ALU.add)
                    op.then_inc(s_d2.h, 1)
                s_d2.n = bd2 + 2 + NDB

            if tap and tap.startswith("m_") and pp == 0 and c == 0:
                with nc.Block() as blk:
                    @blk.sync
                    def _(sync):
                        n16 = 0
                        if tap == "m_xpj":
                            sync.dma_start(out=tap_h[:, :, 0:LC + 3],
                                           in_=xproj[:]).then_inc(s_x.h, 16)
                            n16 = 16
                        elif tap == "m_dt":
                            sync.dma_start(out=tap_h[:, :, 0:LC],
                                           in_=dt_sb[:]).then_inc(s_x.h, 16)
                            n16 = 16
                        elif tap == "m_u":
                            sync.dma_start(out=tap_h[:, :, 0:LC],
                                           in_=u_sb[:]).then_inc(s_x.h, 16)
                            n16 = 16
                        elif tap == "m_zs":
                            sync.dma_start(out=tap_h[:, :, 0:LC],
                                           in_=zs_sb[:]).then_inc(s_x.h, 16)
                            n16 = 16
                        sync.wait_ge(s_x.h, s_x.n + n16)
                    if tap in ("m_xpj", "m_dt", "m_u", "m_zs"):
                        s_x.n += 16

                    @blk.gpsimd
                    def _(gp):
                        if tap == "m_BC":
                            gp.dma_start(out=tap_h[0:16, 0, 0:LC],
                                         in_=B_bf[:]).then_inc(s_w.h, 16)
                            gp.dma_start(out=tap_h[0:16, 1, 0:LC],
                                         in_=C_bf[:]).then_inc(s_w.h, 16)
                            gp.wait_ge(s_w.h, s_w.n + 32)
                    if tap == "m_BC":
                        s_w.n += 32

            with nc.named_scope("mB_%d_%d" % (pp, c)), \
                    nc.Block() as blk, contextlib.ExitStack() as sc:
                pc = sc.enter_context
                ps_bc = pc(nc.psum_tensor("psB_%d_%d" % (pp, c), [P, 4, 512], F32))

                bp2, bev, bd2, be2, bpl = (s_p2.n, s_ev.n, s_d2.n, s_e2.n,
                                           s_pl.n)

                @blk.tensor
                def _(t):
                    for n in range(NST):
                        nsl = n % 2
                        if n >= 2:
                            t.wait_ge(s_e2.h, be2 + n - 1)
                        t.matmul(ps_bc[:, nsl * 2, 0:LC], pick16[:, n, :],
                                 B_bf[:], start=True, stop=True)
                        t.matmul(ps_bc[:, nsl * 2 + 1, 0:LC], pick16[:, n, :],
                                 C_bf[:], start=True, stop=True
                                 ).then_inc(s_p2.h, 1)
                s_p2.n = bp2 + NST

                @blk.scalar
                def _(s):
                    for n in range(NST):
                        nsl = n % 2
                        s.wait_ge(s_p2.h, bp2 + n + 1)
                        if n >= 2:
                            s.wait_ge(s_pl.h, bpl + (n - 2) * NDB + NDB)
                        s.activation(Bb_sb[:, nsl, :], ps_bc[:, nsl * 2, 0:LC],
                                     AF.Identity)
                        s.activation(Cb_sb[:, nsl, :],
                                     ps_bc[:, nsl * 2 + 1, 0:LC],
                                     AF.Identity).then_inc(s_e2.h, 1)
                        for db in range(NDB):
                            ii = n * NDB + db
                            sl = ii % 2
                            if ii >= 2:
                                s.wait_ge(s_d2.h, bd2 + (ii - 2) * 3 + 3)
                            s.activation(a_bufs[:, sl, :], dt_sb[:, db, :],
                                         AF.Exp, scale=A_sb[:, db, n:n + 1])
                            if fence:
                                s.drain()
                            s.activation(am1_bufs[:, sl, :], a_bufs[:, sl, :],
                                         AF.Identity,
                                         scale=invA_sb[:, db, n:n + 1],
                                         bias=ninvA_sb[:, db, n:n + 1]
                                         ).then_inc(s_ev.h, 1)
                s_ev.n = bev + NST * NDB
                s_e2.n = be2 + NST

                @blk.vector
                def _(v):
                    for n in range(NST):
                        nsl = n % 2
                        v.wait_ge(s_e2.h, be2 + n + 1)
                        for db in range(NDB):
                            ii = n * NDB + db
                            sl = ii % 2
                            v.wait_ge(s_ev.h, bev + ii + 1)
                            if ii >= 2:
                                v.wait_ge(s_pl.h, bpl + ii - 1)
                            v.tensor_tensor(w2_bufs[:, sl, :],
                                            u_sb[:, db, :],
                                            Bb_sb[:, nsl, :], ALU.mult)
                            if fence:
                                v.drain()
                            v.tensor_tensor(w2_bufs[:, sl, :],
                                            am1_bufs[:, sl, :],
                                            w2_bufs[:, sl, :], ALU.mult)
                            if fence:
                                v.drain()
                            v.tensor_tensor_scan(h_bufs[:, sl, :],
                                                 a_bufs[:, sl, :],
                                                 w2_bufs[:, sl, :],
                                                 h_state[:, db, n:n + 1],
                                                 ALU.mult, ALU.add
                                                 ).then_inc(s_d2.h, 3)
                s_d2.n = bd2 + NST * NDB * 3

                @blk.gpsimd
                def _(gp):
                    for n in range(NST):
                        nsl = n % 2
                        for db in range(NDB):
                            ii = n * NDB + db
                            sl = ii % 2
                            gp.wait_ge(s_d2.h, bd2 + ii * 3 + 3)
                            if fence:
                                gp.drain()
                            gp.tensor_tensor(ch_bufs[:, 0, :],
                                             h_bufs[:, sl, :],
                                             Cb_sb[:, nsl, :], ALU.mult)
                            if fence:
                                gp.drain()
                            gp.tensor_copy(h_state[:, db, n:n + 1],
                                           h_bufs[:, sl, LC - 1:LC])
                            if n == 0:
                                gp.tensor_copy(y_sb[:, db, :],
                                               ch_bufs[:, 0, :]
                                               ).then_inc(s_pl.h, 1)
                            else:
                                gp.tensor_tensor(y_sb[:, db, :],
                                                 y_sb[:, db, :],
                                                 ch_bufs[:, 0, :], ALU.add
                                                 ).then_inc(s_pl.h, 1)
                    gp.drain()
                s_pl.n = bpl + NST * NDB

            if tap == "m_y" and pp == 0 and c == 0:
                with nc.Block() as blk:
                    @blk.sync
                    def _(sync):
                        sync.dma_start(out=tap_h[:, :, 0:LC],
                                       in_=y_sb[:]).then_inc(s_x.h, 16)
                        sync.wait_ge(s_x.h, s_x.n + 16)
                    s_x.n += 16

            with nc.named_scope("mC_%d_%d" % (pp, c)), \
                    nc.Block() as blk, contextlib.ExitStack() as sc:
                pc = sc.enter_context
                ps_wo = pc(nc.psum_tensor("psC_%d_%d" % (pp, c), [P, 2, 512], F32))

                bw, bp2, bd2 = s_w.n, s_p2.n, s_d2.n

                @blk.vector
                def _(v):
                    op = None
                    for db in range(NDB):
                        if fence:
                            v.drain()
                        v.scalar_tensor_tensor(ch_bufs[:, 0, :], u_sb[:, db, :],
                                               dp_sb[:, db:db + 1],
                                               y_sb[:, db, :], ALU.mult,
                                               ALU.add)
                        if fence:
                            v.drain()
                        op = v.tensor_tensor(xn[:, db, t0:t0 + LC],
                                             ch_bufs[:, 0, :],
                                             zs_sb[:, db, :], ALU.mult)
                    op.then_inc(s_d2.h, 1)
                    for db in range(NDB):
                        sl = db % 2
                        v.wait_ge(s_p2.h, bp2 + db + 1)
                        v.tensor_tensor(xr[:, db, t0:t0 + LC],
                                        xr[:, db, t0:t0 + LC],
                                        ps_wo[:, sl, 0:LC], ALU.add
                                        ).then_inc(s_d2.h, 1)
                s_d2.n = bd2 + 1 + NDB

                @blk.tensor
                def _(t):
                    t.wait_ge(s_d2.h, bd2 + 1)
                    for db in range(NDB):
                        sl = db % 2
                        if db >= 2:
                            t.wait_ge(s_d2.h, bd2 + 1 + db - 1)
                        mm = None
                        for k in range(NDB):
                            mm = t.matmul(ps_wo[:, sl, 0:LC],
                                          Wo_sb[:, k, db * P:(db + 1) * P],
                                          xn[:, k, t0:t0 + LC],
                                          start=(k == 0), stop=(k == NDB - 1))
                        mm.then_inc(s_p2.h, 1)
                s_p2.n = bp2 + NDB


        def emit_store(b, hh):
            with nc.Block() as blk:
                bx = s_x.n

                @blk.sync
                def _(sync):
                    dst = out_h[b].rearrange("(db p) l -> p db l", p=P)[
                        :, :, hh * LP:(hh + 1) * LP]
                    sync.dma_start(out=dst, in_=xr[:]).then_inc(s_x.h, 16)
                    sync.wait_ge(s_x.h, bx + 16)
                s_x.n = bx + 16

        for pp in range(npass):
            b, hh = divmod(pp, NH)
            if hh == 0 and do_mamba:
                with nc.Block() as blk:
                    @blk.vector
                    def _(v):
                        v.memset(h_state[:], 0.0)
                        v.drain()

            def ffn_call(tag, xin_dma=None, first_call=False, split=False):
                with contextlib.ExitStack() as fs:
                    fc = fs.enter_context
                    sg = fc(nc.sbuf_tensor("sgw_%s" % tag, [P, NDB, 1408], BF16))
                    su = fc(nc.sbuf_tensor("suw_%s" % tag, [P, NDB, 1408], BF16))
                    sd = fc(nc.sbuf_tensor("sdw_%s" % tag, [P, NJ, DM], BF16))
                    inter = fc(nc.sbuf_tensor("int_%s" % tag, [P, NJ, LP], BF16))
                    wbs = fc(nc.sbuf_tensor("wbs_%s" % tag, [P, 7, LP], F32))

                    def prefetch(fc0):
                        with nc.Block() as pblk:
                            @pblk.gpsimd
                            def _(gp):
                                pg, pu, pd = ((gT_h, uT_h, dT_h) if fc0
                                              else (scrg_h, scru_h, scrd_h))
                                gsl = pg[0].rearrange(
                                    "(db p) i -> p db i", p=P)[:, :, 0:1408]
                                usl = pu[0].rearrange(
                                    "(db p) i -> p db i", p=P)[:, :, 0:1408]
                                dsl = pd[0].rearrange(
                                    "(jj p) dd -> p jj dd", p=P)[:, 0:NJ, :]
                                gp.dma_start(out=sg[:], in_=gsl
                                             ).then_inc(s_w.h, 16)
                                gp.dma_start(out=su[:], in_=usl
                                             ).then_inc(s_w.h, 16)
                                gp.dma_start(out=sd[:], in_=dsl
                                             ).then_inc(s_w.h, 16)
                            s_w.n += 48

                    if split:
                        for hw in range(2):
                            fc0 = first_call and hw == 0
                            prefetch(fc0)
                            emit_ln("%s_h%d" % (tag, hw),
                                    xin_dma=(xin_dma if hw == 0 else None),
                                    tok0=hw * 512, ntok=512)
                            emit_router("%s_h%d" % (tag, hw), wbs,
                                        tok0=hw * 512, ntok=512)
                            emit_ffn("%s_h%d" % (tag, hw), wbs, sg, su, sd,
                                     inter, skip_first_load=True,
                                     first_call=fc0, tok0=hw * 512, ntok=512)
                        return
                    prefetch(first_call)
                    emit_ln(tag, xin_dma=xin_dma)
                    emit_router(tag, wbs)
                    if tap == "wbs" and tag.endswith("_0") and pp == 0:
                        with nc.Block() as blk2:
                            @blk2.sync
                            def _(sync):
                                for e in range(7):
                                    sync.dma_start(out=tap_h[:, e, :],
                                                   in_=wbs[:, e, :]
                                                   ).then_inc(s_x.h, 16)
                                sync.wait_ge(s_x.h, s_x.n + 112)
                            s_x.n += 112
                    emit_ffn(tag, wbs, sg, su, sd, inter,
                             skip_first_load=True, first_call=first_call)

            if not do_ffn1:
                with nc.named_scope("p%d_ln1" % pp):
                    emit_ln("%d_0" % pp, xin_dma=(b, hh))
            if tap and pp == 0:
                with nc.Block() as blk, \
                        nc.sbuf_tensor("tapf", [P, NDB, LP], F32) as tapf:
                    @blk.vector
                    def _(v):
                        for db in range(NDB):
                            v.tensor_copy(tapf[:, db, :], xn[:, db, :])
                        v.drain().then_inc(s_d2.h, 1)
                    s_d2.n += 1

                    @blk.sync
                    def _(sync):
                        sync.wait_ge(s_d2.h, s_d2.n)
                        sync.dma_start(out=tap_h, in_=tapf[:]
                                       ).then_inc(s_x.h, 16)
                        sync.wait_ge(s_x.h, s_x.n + 16)
                    s_x.n += 16
            if do_ffn1:
                with nc.named_scope("p%d_ffn1" % pp):
                    ffn_call("%d_0" % pp, xin_dma=(b, hh),
                             first_call=(pp == 0))
            if do_mamba:
                with nc.named_scope("p%d_ln2" % pp):
                    emit_ln("%d_1" % pp)
                with nc.named_scope("p%d_mamba" % pp):
                    emit_mamba(pp, hh)
            if do_ffn2:
                with nc.named_scope("p%d_ffn2" % pp):
                    ffn_call("%d_2" % pp, split=split2)
            with nc.named_scope("p%d_store" % pp):
                emit_store(b, hh)

    return nc


def _host_layout(inputs):
    f32 = np.float32
    x = np.ascontiguousarray(
        np.asarray(inputs["L0_attn_x"], dtype=f32).transpose(0, 2, 1))
    wxT = np.ascontiguousarray(np.asarray(inputs["L0_attn_Wx"], f32).T)
    wzT = np.ascontiguousarray(np.asarray(inputs["L0_attn_Wz"], f32).T)
    woT = np.ascontiguousarray(np.asarray(inputs["L0_attn_Wo"], f32).T)
    wpT = np.ascontiguousarray(np.asarray(inputs["L0_attn_Wp"], f32).T)
    wdtT = np.ascontiguousarray(np.asarray(inputs["L0_attn_Wdt"], f32).T)
    bdt = np.ascontiguousarray(
        np.asarray(inputs["L0_attn_bdt"], f32).reshape(NDB, P).T)
    alog = np.ascontiguousarray(
        np.asarray(inputs["L0_attn_A_log"], f32).reshape(NDB, P, NST)
        .transpose(1, 0, 2))
    dp = np.ascontiguousarray(
        np.asarray(inputs["L0_attn_Dp"], f32).reshape(NDB, P).T)
    convw = np.ascontiguousarray(
        np.asarray(inputs["L0_attn_conv_w"], f32)[:, 0, :]
        .reshape(NDB, P, 4).transpose(1, 0, 2))
    sgT = np.asarray(inputs["L0_ffn_sg"], f32).T
    suT = np.asarray(inputs["L0_ffn_su"], f32).T
    sdT = np.asarray(inputs["L0_ffn_sd"], f32).T
    rgT = np.asarray(inputs["L0_ffn_rg"], f32).transpose(0, 2, 1)
    ruT = np.asarray(inputs["L0_ffn_ru"], f32).transpose(0, 2, 1)
    rdT = np.asarray(inputs["L0_ffn_rd"], f32).transpose(0, 2, 1)
    gT = np.ascontiguousarray(np.concatenate([sgT[None], rgT], axis=0))
    uT = np.ascontiguousarray(np.concatenate([suT[None], ruT], axis=0))
    dT = np.ascontiguousarray(np.concatenate([sdT[None], rdT], axis=0))
    wrT = np.ascontiguousarray(np.asarray(inputs["L0_ffn_wr"], f32).T)
    rb = np.ascontiguousarray(
        np.asarray(inputs["L0_ffn_rb"], f32).reshape(7, 1))
    shared = dict(wxT=wxT, wzT=wzT, woT=woT, wpT=wpT, wdtT=wdtT, bdt=bdt,
                  alog=alog, dp=dp, convw=convw, gT=gT, uT=uT, dT=dT,
                  wrT=wrT, rb=rb)
    return x, shared


def kernel(**inputs):
    x, shared = _host_layout(inputs)
    nc = build()
    in_maps = []
    for c in range(8):
        m = dict(shared)
        m["x"] = np.ascontiguousarray(x[2 * c:2 * c + 2])
        in_maps.append(m)
    res = run_bass_kernel_spmd(nc, in_maps, core_ids=list(range(8)))
    outs = [res.results[c]["out"] for c in range(8)]
    full = np.concatenate(outs, axis=0)
    return np.ascontiguousarray(full.transpose(0, 2, 1)).astype(np.float32)


# version 28
# speedup vs baseline: 1.0237x; 1.0237x over previous
import contextlib
import numpy as np

import concourse.bass as bass
import concourse.mybir as mybir
from concourse.bass_utils import run_bass_kernel_spmd

F32 = mybir.dt.float32
BF16 = mybir.dt.bfloat16
AF = mybir.ActivationFunctionType
ALU = mybir.AluOpType
AXX = mybir.AxisListType.X

P = 128
NDB = 8
DM = 1024
LSEQ = 2048
NB = 2
NH = 2
LP = 1024
LC = 512
NCH = 2
II = 2816
NJ = 11
NIC = 2
NE = 8
NST = 16
RDT = 64
EPS = 1e-5
AEPS = 1e-8


class Ctr:
    __slots__ = ("h", "n")

    def __init__(self, h):
        self.h = h
        self.n = 0


def build(cfg=None):
    cfg = dict(cfg or {})
    npass = cfg.get("npass", NB * NH)
    do_ffn1 = cfg.get("ffn1", True)
    do_mamba = cfg.get("mamba", True)
    do_ffn2 = cfg.get("ffn2", True)
    n_exp = cfg.get("ne", NE)
    split2 = cfg.get("split2", False)
    fence = cfg.get("fence", False)

    tap = cfg.get("tap")
    nc = bass.Bass("TRN2", target_bir_lowering=False)

    x_h = nc.dram_tensor("x", [NB, DM, LSEQ], F32, kind="ExternalInput").ap()
    wxT_h = nc.dram_tensor("wxT", [DM, DM], F32, kind="ExternalInput").ap()
    wzT_h = nc.dram_tensor("wzT", [DM, DM], F32, kind="ExternalInput").ap()
    woT_h = nc.dram_tensor("woT", [DM, DM], F32, kind="ExternalInput").ap()
    wpT_h = nc.dram_tensor("wpT", [DM, 96], F32, kind="ExternalInput").ap()
    wdtT_h = nc.dram_tensor("wdtT", [RDT, DM], F32, kind="ExternalInput").ap()
    bdt_h = nc.dram_tensor("bdt", [P, NDB], F32, kind="ExternalInput").ap()
    alog_h = nc.dram_tensor("alog", [P, NDB, NST], F32, kind="ExternalInput").ap()
    dp_h = nc.dram_tensor("dp", [P, NDB], F32, kind="ExternalInput").ap()
    convw_h = nc.dram_tensor("convw", [P, NDB, 4], F32, kind="ExternalInput").ap()
    gT_h = nc.dram_tensor("gT", [NE, DM, II], F32, kind="ExternalInput").ap()
    uT_h = nc.dram_tensor("uT", [NE, DM, II], F32, kind="ExternalInput").ap()
    dT_h = nc.dram_tensor("dT", [NE, II, DM], F32, kind="ExternalInput").ap()
    wrT_h = nc.dram_tensor("wrT", [DM, 7], F32, kind="ExternalInput").ap()
    rb_h = nc.dram_tensor("rb", [7, 1], F32, kind="ExternalInput").ap()
    out_h = nc.dram_tensor("out", [NB, DM, LSEQ], F32, kind="ExternalOutput").ap()
    scrg_h = nc.dram_tensor("scrg", [NE, DM, II], BF16, kind="Internal").ap()
    scru_h = nc.dram_tensor("scru", [NE, DM, II], BF16, kind="Internal").ap()
    scrd_h = nc.dram_tensor("scrd", [NE, II, DM], BF16, kind="Internal").ap()
    tap_h = None
    if tap:
        tap_h = nc.dram_tensor("tap", [P, NDB, LP], F32,
                               kind="ExternalOutput").ap()

    es = contextlib.ExitStack()
    with es:
        ec = es.enter_context

        xr = ec(nc.sbuf_tensor("xr", [P, NDB, LP], F32))
        xn = ec(nc.sbuf_tensor("xn", [P, NDB, LP], BF16))
        identf = ec(nc.sbuf_tensor("identf", [P, P], F32))
        pick16 = ec(nc.sbuf_tensor("pick16", [16, 16, P], BF16))
        pickf = ec(nc.sbuf_tensor("pickf", [16, 7, P], F32))
        ones_col = ec(nc.sbuf_tensor("ones_col", [P, 1], F32))
        ones_bf = ec(nc.sbuf_tensor("ones_bf", [P, 1], BF16))
        ones_row = ec(nc.sbuf_tensor("ones_row", [1, P], F32))
        bdt_sb = ec(nc.sbuf_tensor("bdt_sb", [P, NDB], F32))
        dp_sb = ec(nc.sbuf_tensor("dp_sb", [P, NDB], F32))
        convw_sb = ec(nc.sbuf_tensor("convw_sb", [P, NDB, 4], F32))
        alog_sb = ec(nc.sbuf_tensor("alog_sb", [P, NDB, NST], F32))
        A_sb = ec(nc.sbuf_tensor("A_sb", [P, NDB, NST], F32))
        invA_sb = ec(nc.sbuf_tensor("invA_sb", [P, NDB, NST], F32))
        ninvA_sb = ec(nc.sbuf_tensor("ninvA_sb", [P, NDB, NST], F32))
        wrT_sb = ec(nc.sbuf_tensor("wrT_sb", [P, NDB, 7], BF16))
        rb_sb = ec(nc.sbuf_tensor("rb_sb", [7, 1], F32))
        wdtT_sb = ec(nc.sbuf_tensor("wdtT_sb", [RDT, DM], BF16))
        wpT_sb = ec(nc.sbuf_tensor("wpT_sb", [P, NDB, 96], BF16))
        h_state = ec(nc.sbuf_tensor("h_state", [P, NDB, NST], F32))
        pad_save = ec(nc.sbuf_tensor("pad_save", [P, NDB, 3], BF16))
        cst = ec(nc.sbuf_tensor("cst", [P, 4], F32))
        _cvals = [EPS, 1.0, -1.0, -0.5]
        for _i, _v in enumerate(_cvals):
            nc.const_aps.register(F32, _v, cst[:, _i:_i + 1])

        s_x = Ctr(ec(nc.semaphore("s_x")))
        s_w = Ctr(ec(nc.semaphore("s_w")))
        s_ps = Ctr(ec(nc.semaphore("s_ps")))
        s_in = Ctr(ec(nc.semaphore("s_in")))
        s_ev = Ctr(ec(nc.semaphore("s_ev")))
        s_dn = Ctr(ec(nc.semaphore("s_dn")))
        s_rs = Ctr(ec(nc.semaphore("s_rs")))
        s_p2 = Ctr(ec(nc.semaphore("s_p2")))
        s_d2 = Ctr(ec(nc.semaphore("s_d2")))
        s_e2 = Ctr(ec(nc.semaphore("s_e2")))
        s_pl = Ctr(ec(nc.semaphore("s_pl")))

        with nc.Block() as blk:
            @blk.sync
            def _(sync):
                sync.dma_start(out=bdt_sb[:], in_=bdt_h).then_inc(s_x.h, 16)
                sync.dma_start(out=dp_sb[:], in_=dp_h).then_inc(s_x.h, 16)
                sync.dma_start(out=convw_sb[:], in_=convw_h).then_inc(s_x.h, 16)
                sync.dma_start(out=alog_sb[:], in_=alog_h).then_inc(s_x.h, 16)
                sync.dma_start(out=rb_sb[:], in_=rb_h).then_inc(s_x.h, 16)
            s_x.n += 80

            @blk.gpsimd
            def _(gp):
                gp.dma_start(out=wrT_sb[:],
                             in_=wrT_h.rearrange("(db p) e -> p db e", p=P)
                             ).then_inc(s_w.h, 16)
                gp.dma_start(out=wdtT_sb[:], in_=wdtT_h).then_inc(s_w.h, 16)
                gp.dma_start(out=wpT_sb[:],
                             in_=wpT_h.rearrange("(db p) m -> p db m", p=P)
                             ).then_inc(s_w.h, 16)
            s_w.n += 48

        with nc.Block() as blk, \
                nc.sbuf_tensor("iota_a", [P, P], F32) as iota_a, \
                nc.sbuf_tensor("iota_b", [P, P], F32) as iota_b, \
                nc.sbuf_tensor("piota_a", [16, 16, P], F32) as piota_a, \
                nc.sbuf_tensor("piota_b", [16, 16, P], F32) as piota_b:
            @blk.gpsimd
            def _(gp):
                gp.iota(iota_a[:], [[0, P]], channel_multiplier=1,
                        allow_small_or_imprecise_dtypes=True)
                gp.iota(iota_b[:], [[1, P]], channel_multiplier=0,
                        allow_small_or_imprecise_dtypes=True)
                gp.iota(piota_a[:], [[0, 16], [0, P]], channel_multiplier=1,
                        allow_small_or_imprecise_dtypes=True)
                gp.iota(piota_b[:], [[1, 16], [0, P]], channel_multiplier=0,
                        allow_small_or_imprecise_dtypes=True
                        ).then_inc(s_pl.h, 1)
            s_pl.n += 1

            @blk.vector
            def _(v):
                v.wait_ge(s_pl.h, s_pl.n)
                v.tensor_tensor(identf[:], iota_a[:], iota_b[:], ALU.is_equal)
                v.tensor_tensor(pick16[:], piota_a[:], piota_b[:], ALU.is_equal)
                v.tensor_tensor(pickf[:], piota_a[:, 0:7, :],
                                piota_b[:, 0:7, :], ALU.is_equal)
                v.memset(ones_col[:], 1.0)
                v.memset(ones_bf[:], 1.0)
                v.memset(ones_row[:], 1.0)
                v.memset(h_state[:], 0.0)
                v.memset(pad_save[:], 0.0)
                for _i, _v in enumerate(_cvals):
                    v.memset(cst[:, _i:_i + 1], _v)
                v.drain()

            @blk.scalar
            def _(s):
                s.wait_ge(s_x.h, s_x.n)
                s.activation(A_sb[:], alog_sb[:], AF.Exp).then_inc(s_ev.h, 1)
            s_ev.n += 1

        with nc.Block() as blk:
            for _reg in (blk.tensor, blk.vector, blk.scalar, blk.gpsimd,
                         blk.sync):
                def _body(eng, _x=s_x.n, _w=s_w.n):
                    eng.wait_ge(s_x.h, _x)
                    eng.wait_ge(s_w.h, _w)
                _reg(_body)

        with nc.Block() as blk:
            @blk.vector
            def _(v):
                v.wait_ge(s_ev.h, s_ev.n)
                v.tensor_single_scalar(A_sb[:], A_sb[:], -1.0, ALU.mult)
                v.drain()
                v.tensor_single_scalar(invA_sb[:], A_sb[:], AEPS, ALU.add)
                v.drain()
                v.reciprocal(invA_sb[:], invA_sb[:])
                v.drain()
                v.tensor_single_scalar(ninvA_sb[:], invA_sb[:], -1.0, ALU.mult)
                v.drain()

        def emit_ln(tag, xin_dma=None, tok0=0, ntok=LP):
            NLCW = ntok // 512
            with nc.Block() as blk, contextlib.ExitStack() as sc:
                pc = sc.enter_context
                ps_s1 = pc(nc.psum_tensor("ps_s1_%s" % tag, [1, ntok], F32))
                ps_s2 = pc(nc.psum_tensor("ps_s2_%s" % tag, [1, ntok], F32))
                ps_b1 = pc(nc.psum_tensor("ps_b1_%s" % tag, [P, 2, 512], F32))
                xsq = pc(nc.sbuf_tensor("xsq_%s" % tag, [P, 2, 512], BF16))
                stat = pc(nc.sbuf_tensor("stat_%s" % tag, [1, 4, ntok], F32))
                tmpf = pc(nc.sbuf_tensor("tmpf_%s" % tag, [P, 512], F32))

                bev, bp2, bd2, be2 = s_ev.n, s_p2.n, s_d2.n, s_e2.n
                NSQ = NLCW * NDB

                def W(lc):
                    return slice(tok0 + lc * 512, tok0 + (lc + 1) * 512)

                if xin_dma is not None:
                    @blk.sync
                    def _(sync):
                        b, hh = xin_dma
                        srcx = x_h[b].rearrange("(db p) l -> p db l", p=P)[
                            :, :, hh * LP:(hh + 1) * LP]
                        sync.dma_start(out=xr[:], in_=srcx).then_inc(s_x.h, 16)
                    s_x.n += 16

                @blk.scalar
                def _(s):
                    s.wait_ge(s_x.h, s_x.n)
                    for db in range(NDB):
                        s.activation(xn[:, db, tok0:tok0 + ntok],
                                     xr[:, db, tok0:tok0 + ntok], AF.Identity)
                    s.drain().then_inc(s_e2.h, 1)
                    k = 0
                    for lc in range(NLCW):
                        for db in range(NDB):
                            if k >= 2:
                                s.wait_ge(s_p2.h, bp2 + k - 1)
                            s.activation(xsq[:, k % 2, :], xr[:, db, W(lc)],
                                         AF.Square).then_inc(s_ev.h, 1)
                            k += 1
                    s.wait_ge(s_d2.h, bd2 + 1)
                    s.activation(stat[:, 1, :], stat[:, 1, :], AF.Ln, bias=EPS)
                    s.drain()
                    s.activation(stat[:, 2, :], stat[:, 1, :], AF.Exp,
                                 scale=-0.5)
                    s.drain().then_inc(s_e2.h, 1)
                s_ev.n = bev + NSQ
                s_e2.n = be2 + 2

                @blk.tensor
                def _(t):
                    t.wait_ge(s_e2.h, be2 + 1)
                    for lc in range(NLCW):
                        for db in range(NDB):
                            t.matmul(ps_s1[0:1, lc * 512:(lc + 1) * 512],
                                     ones_bf[:], xn[:, db, W(lc)],
                                     start=(db == 0), stop=(db == NDB - 1))
                    k = 0
                    for lc in range(NLCW):
                        for db in range(NDB):
                            t.wait_ge(s_ev.h, bev + k + 1)
                            t.matmul(ps_s2[0:1, lc * 512:(lc + 1) * 512],
                                     ones_bf[:], xsq[:, k % 2, :],
                                     start=(db == 0), stop=(db == NDB - 1)
                                     ).then_inc(s_p2.h, 1)
                            k += 1
                    t.wait_ge(s_d2.h, bd2 + 2)
                    for lc in range(NLCW):
                        if lc >= 1:
                            t.wait_ge(s_d2.h, bd2 + 2 + lc)
                        t.matmul(ps_b1[:, 0, :], ones_row[:],
                                 stat[0:1, 2, lc * 512:(lc + 1) * 512],
                                 start=True, stop=True)
                        t.matmul(ps_b1[:, 1, :], ones_row[:],
                                 stat[0:1, 3, lc * 512:(lc + 1) * 512],
                                 start=True, stop=True).then_inc(s_p2.h, 1)
                s_p2.n = bp2 + NSQ + NLCW

                @blk.vector
                def _(v):
                    v.wait_ge(s_p2.h, bp2 + NSQ)
                    v.tensor_single_scalar(stat[:, 0, :], ps_s1[0:1, :],
                                           1.0 / DM, ALU.mult)
                    v.drain()
                    v.tensor_tensor(stat[:, 1, :], stat[:, 0, :],
                                    stat[:, 0, :], ALU.mult)
                    v.drain()
                    v.scalar_tensor_tensor(stat[:, 1, :], ps_s2[0:1, :],
                                           1.0 / DM, stat[:, 1, :], ALU.mult,
                                           ALU.subtract)
                    v.drain().then_inc(s_d2.h, 1)
                    v.wait_ge(s_e2.h, be2 + 2)
                    v.scalar_tensor_tensor(stat[:, 3, :], stat[:, 0, :], -1.0,
                                           stat[:, 2, :], ALU.mult, ALU.mult)
                    v.drain().then_inc(s_d2.h, 1)
                    for lc in range(NLCW):
                        v.wait_ge(s_p2.h, bp2 + NSQ + lc + 1)
                        op = None
                        for db in range(NDB):
                            if fence:
                                v.drain()
                            v.tensor_tensor(tmpf[:], xr[:, db, W(lc)],
                                            ps_b1[:, 0, :], ALU.mult)
                            if fence:
                                v.drain()
                            op = v.tensor_tensor(xn[:, db, W(lc)],
                                                 tmpf[:], ps_b1[:, 1, :],
                                                 ALU.add)
                        op.then_inc(s_d2.h, 1)
                s_d2.n = bd2 + 2 + NLCW

        def emit_router(tag, wbs, tok0=0, ntok=LP):
            NLCW = ntok // 512
            NTT = ntok // P
            with contextlib.ExitStack() as sc:
                pc = sc.enter_context
                ps_r = pc(nc.psum_tensor("ps_r_%s" % tag, [7, 2, 512], F32))
                lt_ps = pc(nc.psum_tensor("lt_ps_%s" % tag, [P, NTT, 7], F32))
                wcm_ps = pc(nc.psum_tensor("wcm_ps_%s" % tag, [7, ntok], F32))
                wb_ps = pc(nc.psum_tensor("wb_ps_%s" % tag, [P, 2, 512], F32))
                lg_cm = pc(nc.sbuf_tensor("lg_cm_%s" % tag, [7, ntok], F32))
                lg_tm = pc(nc.sbuf_tensor("lg_tm_%s" % tag, [P, NTT, 7], F32))
                e_tm = pc(nc.sbuf_tensor("e_tm_%s" % tag, [P, NTT, 7], F32))
                w_tm = pc(nc.sbuf_tensor("w_tm_%s" % tag, [P, NTT, 7], F32))
                mstat = pc(nc.sbuf_tensor("mstat_%s" % tag, [P, 5, NTT], F32))
                mk = pc(nc.sbuf_tensor("mk_%s" % tag, [P, NTT, 7], F32))
                em = pc(nc.sbuf_tensor("em_%s" % tag, [P, NTT, 7], F32))
                w_cm = pc(nc.sbuf_tensor("w_cm_%s" % tag, [7, ntok], F32))

                bev, bp2, bd2, be2 = s_ev.n, s_p2.n, s_d2.n, s_e2.n

                def W(lc):
                    return slice(tok0 + lc * 512, tok0 + (lc + 1) * 512)

                with nc.Block() as blk:
                    @blk.tensor
                    def _(t):
                        for lc in range(NLCW):
                            for db in range(NDB):
                                t.matmul(ps_r[:, lc % 2, :], wrT_sb[:, db, :],
                                         xn[:, db, W(lc)],
                                         start=(db == 0), stop=(db == NDB - 1)
                                         ).then_inc(
                                             s_p2.h, 1 if db == NDB - 1 else 0)
                        t.wait_ge(s_ev.h, bev + NLCW)
                        for tt in range(NTT):
                            t.transpose(lt_ps[:, tt, :],
                                        lg_cm[0:7, tt * P:(tt + 1) * P],
                                        identf[0:7, 0:7]).then_inc(
                                            s_p2.h, 1 if tt == NTT - 1 else 0)
                        t.wait_ge(s_d2.h, bd2 + 2)
                        for tt in range(NTT):
                            t.transpose(wcm_ps[:, tt * P:(tt + 1) * P],
                                        w_tm[:, tt, :], identf[:]).then_inc(
                                            s_p2.h, 1 if tt == NTT - 1 else 0)
                        t.wait_ge(s_e2.h, be2 + 3)
                        k = 0
                        for e in range(7):
                            for lc in range(NLCW):
                                if k >= 2:
                                    t.wait_ge(s_ev.h, bev + NLCW + k - 1)
                                t.matmul(wb_ps[:, k % 2, :], pickf[0:7, e, :],
                                         w_cm[:, lc * 512:(lc + 1) * 512],
                                         start=True, stop=True
                                         ).then_inc(s_p2.h, 1)
                                k += 1
                    s_p2.n = bp2 + NLCW + 2 + 7 * NLCW

                    @blk.scalar
                    def _(s):
                        for lc in range(NLCW):
                            s.wait_ge(s_p2.h, bp2 + lc + 1)
                            s.activation(lg_cm[0:7, lc * 512:(lc + 1) * 512],
                                         ps_r[:, lc % 2, :], AF.Identity,
                                         bias=rb_sb[:, 0:1]).then_inc(s_ev.h, 1)
                        s.wait_ge(s_p2.h, bp2 + NLCW + 1)
                        s.activation(lg_tm[:], lt_ps[:], AF.Identity
                                     ).then_inc(s_e2.h, 1)
                        s.wait_ge(s_d2.h, bd2 + 1)
                        op = None
                        for tt in range(NTT):
                            op = s.activation(e_tm[:, tt, :], lg_tm[:, tt, :],
                                              AF.Exp,
                                              bias=mstat[:, 1, tt:tt + 1])
                        op.then_inc(s_e2.h, 1)
                        s.wait_ge(s_p2.h, bp2 + NLCW + 2)
                        s.activation(w_cm[:], wcm_ps[0:7, :], AF.Identity
                                     ).then_inc(s_e2.h, 1)
                        k = 0
                        for e in range(7):
                            for lc in range(NLCW):
                                s.wait_ge(s_p2.h, bp2 + NLCW + 2 + k + 1)
                                s.activation(wbs[:, e, W(lc)],
                                             wb_ps[:, k % 2, :], AF.Identity
                                             ).then_inc(s_ev.h, 1)
                                k += 1
                    s_ev.n = bev + NLCW + 7 * NLCW
                    s_e2.n = be2 + 3

                    @blk.vector
                    def _(v):
                        v.wait_ge(s_e2.h, be2 + 1)
                        for tt in range(NTT):
                            v.tensor_reduce(mstat[:, 0, tt:tt + 1],
                                            lg_tm[:, tt, :], AXX, ALU.max)
                        v.drain()
                        v.tensor_single_scalar(mstat[:, 1, :], mstat[:, 0, :],
                                               -1.0, ALU.mult)
                        v.drain().then_inc(s_d2.h, 1)
                        v.wait_ge(s_e2.h, be2 + 2)
                        for tt in range(NTT):
                            v.tensor_reduce(mstat[:, 2, tt:tt + 1],
                                            e_tm[:, tt, :], AXX, ALU.max)
                        v.drain()
                        for tt in range(NTT):
                            v.tensor_scalar(mk[:, tt, :], e_tm[:, tt, :],
                                            mstat[:, 2, tt:tt + 1], None,
                                            ALU.is_lt)
                        v.drain()
                        for tt in range(NTT):
                            v.tensor_tensor(em[:, tt, :], e_tm[:, tt, :],
                                            mk[:, tt, :], ALU.mult)
                        v.drain()
                        for tt in range(NTT):
                            v.tensor_reduce(mstat[:, 3, tt:tt + 1],
                                            em[:, tt, :], AXX, ALU.max)
                        v.drain()
                        v.tensor_tensor(mstat[:, 4, :], mstat[:, 2, :],
                                        mstat[:, 3, :], ALU.add)
                        v.drain()
                        v.reciprocal(mstat[:, 4, :], mstat[:, 4, :])
                        v.drain()
                        for tt in range(NTT):
                            v.tensor_scalar(mk[:, tt, :], e_tm[:, tt, :],
                                            mstat[:, 3, tt:tt + 1], None,
                                            ALU.is_ge)
                        v.drain()
                        for tt in range(NTT):
                            v.tensor_tensor(em[:, tt, :], e_tm[:, tt, :],
                                            mk[:, tt, :], ALU.mult)
                        v.drain()
                        for tt in range(NTT):
                            v.tensor_scalar(w_tm[:, tt, :], em[:, tt, :],
                                            mstat[:, 4, tt:tt + 1], 0.5,
                                            ALU.mult, ALU.mult)
                        v.drain().then_inc(s_d2.h, 1)
                    s_d2.n = bd2 + 2

        def emit_ffn(tag, wbs, sg, su, sd, inter, skip_first_load=False,
                     first_call=False, plan=((0, 11), (11, 11)),
                     tok0=0, ntok=LP):
            with nc.Block() as blk, contextlib.ExitStack() as sc:
                pc = sc.enter_context
                ps_g = pc(nc.psum_tensor("ps_g_%s" % tag, [P, 2, 512], F32))
                ps_u = pc(nc.psum_tensor("ps_u_%s" % tag, [P, 2, 512], F32))
                ps_o = pc(nc.psum_tensor("ps_o_%s" % tag, [P, 2, LP], F32))
                sgact = pc(nc.sbuf_tensor("sgact_%s" % tag, [P, 2, 512], F32))
                tmp = pc(nc.sbuf_tensor("tmpr_%s" % tag, [P, 2, LP], F32))

                bw, bps, bin_, bev, bdn, brs = (s_w.n, s_ps.n, s_in.n,
                                                s_ev.n, s_dn.n, s_rs.n)
                if skip_first_load:
                    bw = s_w.n - 48
                NICP = len(plan)
                NIT = n_exp * NICP
                NLCW = ntok // 512
                jbase = [0]
                for _, jc in plan:
                    jbase.append(jbase[-1] + jc)
                npair = [NLCW * b for b in jbase]
                PPE = npair[-1]

                def WL(lc):
                    return slice(tok0 + lc * 512, tok0 + (lc + 1) * 512)

                def pair0(it):
                    e, ic = divmod(it, NICP)
                    return e * PPE + npair[ic]

                def dcnt(it):
                    return it * NDB

                bx0 = s_x.n

                def _loads(q):
                    gsrc, usrc, dsrc = ((gT_h, uT_h, dT_h) if first_call else
                                        (scrg_h, scru_h, scrd_h))
                    for it in range(1 if skip_first_load else 0, NIT):
                        e, ic = divmod(it, NICP)
                        js, jc = plan[ic]
                        gsl = gsrc[e].rearrange("(db p) i -> p db i", p=P)[
                            :, :, js * P:(js + jc) * P]
                        usl = usrc[e].rearrange("(db p) i -> p db i", p=P)[
                            :, :, js * P:(js + jc) * P]
                        dsl = dsrc[e].rearrange("(jj p) dd -> p jj dd",
                                                p=P)[:, js:js + jc, :]
                        if it > 0:
                            q.wait_ge(s_ps.h, bps + pair0(it))
                            if first_call:
                                q.wait_ge(s_x.h, bx0 + it * 48)
                        q.dma_start(out=sg[:, :, 0:jc * P], in_=gsl
                                    ).then_inc(s_w.h, 16)
                        q.dma_start(out=su[:, :, 0:jc * P], in_=usl
                                    ).then_inc(s_w.h, 16)
                        if it > 0:
                            q.wait_ge(s_dn.h, bdn + dcnt(it))
                            if first_call:
                                q.wait_ge(s_x.h, bx0 + it * 48)
                        q.dma_start(out=sd[:, 0:jc, :], in_=dsl
                                    ).then_inc(s_w.h, 16)

                if first_call:
                    @blk.gpsimd
                    def _(gp):
                        _loads(gp)
                else:
                    @blk.sync
                    def _(sync):
                        _loads(sync)
                s_w.n = bw + NIT * 48

                if first_call:
                    @blk.sync
                    def _(sync):
                        for it in range(NIT):
                            e, ic = divmod(it, NICP)
                            js, jc = plan[ic]
                            gds = scrg_h[e].rearrange("(db p) i -> p db i", p=P)[
                                :, :, js * P:(js + jc) * P]
                            uds = scru_h[e].rearrange("(db p) i -> p db i", p=P)[
                                :, :, js * P:(js + jc) * P]
                            dds = scrd_h[e].rearrange(
                                "(jj p) dd -> p jj dd", p=P)[:, js:js + jc, :]
                            gu_rdy = 48 if it == 0 else 32
                            sync.wait_ge(s_w.h, bw + it * 48 + gu_rdy)
                            sync.dma_start(out=gds, in_=sg[:, :, 0:jc * P]
                                           ).then_inc(s_x.h, 16)
                            sync.dma_start(out=uds, in_=su[:, :, 0:jc * P]
                                           ).then_inc(s_x.h, 16)
                            sync.wait_ge(s_w.h, bw + it * 48 + 48)
                            sync.dma_start(out=dds, in_=sd[:, 0:jc, :]
                                           ).then_inc(s_x.h, 16)
                        sync.wait_ge(s_x.h, bx0 + NIT * 48)
                    s_x.n = bx0 + NIT * 48

                @blk.tensor
                def _(t):
                    for it in range(NIT):
                        e, ic = divmod(it, NICP)
                        js, jc = plan[ic]
                        gu_ready = 48 if (it == 0 and skip_first_load) else 32
                        t.wait_ge(s_w.h, bw + it * 48 + gu_ready)
                        for j in range(jc):
                            for lc in range(NLCW):
                                n = pair0(it) + j * NLCW + lc
                                sl = n % 2
                                if n >= 2:
                                    t.wait_ge(s_in.h, bin_ + n - 1)
                                for k in range(NDB):
                                    t.matmul(ps_g[:, sl, :],
                                             sg[:, k, j * P:(j + 1) * P],
                                             xn[:, k, WL(lc)],
                                             start=(k == 0),
                                             stop=(k == NDB - 1))
                                mm = None
                                for k in range(NDB):
                                    mm = t.matmul(
                                        ps_u[:, sl, :],
                                        su[:, k, j * P:(j + 1) * P],
                                        xn[:, k, WL(lc)],
                                        start=(k == 0), stop=(k == NDB - 1))
                                mm.then_inc(s_ps.h, 1)
                        t.wait_ge(s_w.h, bw + it * 48 + 48)
                        for db in range(NDB):
                            gdb = it * NDB + db
                            sl = db % 2
                            if gdb >= 2:
                                t.wait_ge(s_rs.h, brs + gdb - 1)
                            mm = None
                            for j in range(jc):
                                if db == 0:
                                    t.wait_ge(s_in.h,
                                              bin_ + pair0(it) + (j + 1) * NLCW)
                                for lc in range(NLCW):
                                    mm = t.matmul(
                                        ps_o[:, sl, lc * 512:(lc + 1) * 512],
                                        sd[:, j, db * P:(db + 1) * P],
                                        inter[:, j, lc * 512:(lc + 1) * 512],
                                        start=(j == 0), stop=(j == jc - 1))
                            mm.then_inc(s_dn.h, 1)
                s_ps.n = bps + n_exp * PPE
                s_dn.n = bdn + NIT * NDB

                @blk.scalar
                def _(s):
                    for it in range(NIT):
                        e, ic = divmod(it, NICP)
                        js, jc = plan[ic]
                        for j in range(jc):
                            for lc in range(NLCW):
                                n = pair0(it) + j * NLCW + lc
                                sl = n % 2
                                s.wait_ge(s_ps.h, bps + n + 1)
                                if n >= 2:
                                    s.wait_ge(s_in.h, bin_ + n - 1)
                                s.activation(sgact[:, sl, :], ps_g[:, sl, :],
                                             AF.Silu).then_inc(s_ev.h, 1)
                s_ev.n = bev + n_exp * PPE

                @blk.vector
                def _(v):
                    for it in range(NIT):
                        e, ic = divmod(it, NICP)
                        js, jc = plan[ic]
                        for j in range(jc):
                            for lc in range(NLCW):
                                n = pair0(it) + j * NLCW + lc
                                sl = n % 2
                                v.wait_ge(s_ev.h, bev + n + 1)
                                v.tensor_tensor(
                                    inter[:, j, lc * 512:(lc + 1) * 512],
                                    sgact[:, sl, :], ps_u[:, sl, :],
                                    ALU.mult).then_inc(s_in.h, 1)
                        for db in range(NDB):
                            gdb = it * NDB + db
                            sl = db % 2
                            v.wait_ge(s_dn.h, bdn + gdb + 1)
                            xw = xr[:, db, tok0:tok0 + ntok]
                            if e == 0:
                                op = v.scalar_tensor_tensor(
                                    xw, ps_o[:, sl, 0:ntok], 0.5,
                                    xw, ALU.mult, ALU.add)
                            else:
                                if fence:
                                    v.drain()
                                v.scalar_tensor_tensor(
                                    tmp[:, sl, 0:ntok], ps_o[:, sl, 0:ntok],
                                    1.0, wbs[:, e - 1, tok0:tok0 + ntok],
                                    ALU.bypass, ALU.mult)
                                if fence:
                                    v.drain()
                                op = v.tensor_tensor(xw, xw,
                                                     tmp[:, sl, 0:ntok],
                                                     ALU.add)
                            op.then_inc(s_rs.h, 1)
                s_in.n = bin_ + n_exp * PPE
                s_rs.n = brs + NIT * NDB

        def emit_mamba(pp, hh):
            with contextlib.ExitStack() as mw:
                mc = mw.enter_context
                W1 = mc(nc.sbuf_tensor("W1_%d" % pp, [P, NDB, DM], BF16))
                Wz_sb = mc(nc.sbuf_tensor("Wzs_%d" % pp, [P, NDB, DM], BF16))
                Wo_sb = mc(nc.sbuf_tensor("Wos_%d" % pp, [P, NDB, DM], BF16))
                xproj = mc(nc.sbuf_tensor("xpj_%d" % pp, [P, NDB, LC + 3], BF16))
                dt_sb = mc(nc.sbuf_tensor("dt_%d" % pp, [P, NDB, LC], F32))
                u_sb = mc(nc.sbuf_tensor("u_%d" % pp, [P, NDB, LC], F32))
                zs_sb = mc(nc.sbuf_tensor("zs_%d" % pp, [P, NDB, LC], BF16))
                y_sb = mc(nc.sbuf_tensor("y_%d" % pp, [P, NDB, LC], F32))
                dtun_bf = mc(nc.sbuf_tensor("dtun_%d" % pp, [RDT, LC], BF16))
                B_bf = mc(nc.sbuf_tensor("Bbf_%d" % pp, [16, LC], BF16))
                C_bf = mc(nc.sbuf_tensor("Cbf_%d" % pp, [16, LC], BF16))
                Bb_sb = mc(nc.sbuf_tensor("Bb_%d" % pp, [P, 2, LC], F32))
                Cb_sb = mc(nc.sbuf_tensor("Cb_%d" % pp, [P, 2, LC], F32))
                a_bufs = mc(nc.sbuf_tensor("abuf_%d" % pp, [P, 2, LC], F32))
                am1_bufs = mc(nc.sbuf_tensor("am1_%d" % pp, [P, 2, LC], F32))
                w2_bufs = mc(nc.sbuf_tensor("w2b_%d" % pp, [P, 2, LC], F32))
                h_bufs = mc(nc.sbuf_tensor("hbuf_%d" % pp, [P, 2, LC], F32))
                ch_bufs = mc(nc.sbuf_tensor("chb_%d" % pp, [P, 1, LC], F32))
                tdt = mc(nc.sbuf_tensor("tdt_%d" % pp, [P, 1, LC], F32))

                with nc.Block() as blk:
                    bw = s_w.n

                    @blk.gpsimd
                    def _(gp):
                        gp.dma_start(
                            out=W1[:],
                            in_=wxT_h.rearrange("(kb p) m -> p kb m", p=P)
                        ).then_inc(s_w.h, 16)
                        gp.dma_start(
                            out=Wz_sb[:],
                            in_=wzT_h.rearrange("(kb p) m -> p kb m", p=P)
                        ).then_inc(s_w.h, 16)
                        gp.dma_start(
                            out=Wo_sb[:],
                            in_=woT_h.rearrange("(kb p) m -> p kb m", p=P)
                        ).then_inc(s_w.h, 16)
                    s_w.n = bw + 48

                bufs = (W1, Wz_sb, Wo_sb, xproj, dt_sb, u_sb, zs_sb, y_sb,
                        dtun_bf, B_bf, C_bf, Bb_sb, Cb_sb, a_bufs, am1_bufs,
                        w2_bufs, h_bufs, ch_bufs, tdt)
                for c in range(NCH):
                    with nc.named_scope("p%d_mc%d" % (pp, c)):
                        emit_mamba_chunk(pp, hh, c, bufs)

                if hh == 0:
                    with nc.Block() as blk:
                        @blk.vector
                        def _(v):
                            v.tensor_copy(pad_save[:], xproj[:, :, LC:LC + 3])
                            v.drain()

        def emit_mamba_chunk(pp, hh, c, bufs):
            (W1, Wz_sb, Wo_sb, xproj, dt_sb, u_sb, zs_sb, y_sb, dtun_bf,
             B_bf, C_bf, Bb_sb, Cb_sb, a_bufs, am1_bufs, w2_bufs, h_bufs,
             ch_bufs, tdt) = bufs
            t0 = c * LC

            with nc.named_scope("mA_%d_%d" % (pp, c)), \
                    nc.Block() as blk, contextlib.ExitStack() as sc:
                pc = sc.enter_context
                ps_m = pc(nc.psum_tensor("psA_%d_%d" % (pp, c), [P, 2, 512], F32))
                ps_p1 = pc(nc.psum_tensor("psP1_%d_%d" % (pp, c), [80, LC], F32))
                ps_p2 = pc(nc.psum_tensor("psP2_%d_%d" % (pp, c), [16, LC], F32))

                bw, bp2, bev, bd2, be2 = (s_w.n, s_p2.n, s_ev.n, s_d2.n,
                                          s_e2.n)

                @blk.tensor
                def _(t):
                    t.wait_ge(s_w.h, bw)
                    for db in range(NDB):
                        sl = db % 2
                        if db >= 2:
                            t.wait_ge(s_ev.h, bev + db - 1)
                        for k in range(NDB):
                            t.matmul(ps_m[:, sl, 0:LC],
                                     W1[:, k, db * P:(db + 1) * P],
                                     xn[:, k, t0:t0 + LC],
                                     start=(k == 0), stop=(k == NDB - 1)
                                     ).then_inc(s_p2.h,
                                                1 if k == NDB - 1 else 0)
                    for k in range(NDB):
                        t.matmul(ps_p1[:], wpT_sb[:, k, 0:80],
                                 xn[:, k, t0:t0 + LC],
                                 start=(k == 0), stop=(k == NDB - 1))
                    mm = None
                    for k in range(NDB):
                        mm = t.matmul(ps_p2[:], wpT_sb[:, k, 80:96],
                                      xn[:, k, t0:t0 + LC],
                                      start=(k == 0), stop=(k == NDB - 1))
                    mm.then_inc(s_p2.h, 1)
                    t.wait_ge(s_ev.h, bev + NDB + 1)
                    for db in range(NDB):
                        sl = db % 2
                        if db >= 2:
                            t.wait_ge(s_e2.h, be2 + db - 1)
                        t.matmul(ps_m[:, sl, 0:LC],
                                 wdtT_sb[:, db * P:(db + 1) * P],
                                 dtun_bf[:], start=True, stop=True
                                 ).then_inc(s_p2.h, 1)
                    for db in range(NDB):
                        sl = db % 2
                        if db < 2:
                            t.wait_ge(s_e2.h, be2 + 7 + db)
                        else:
                            t.wait_ge(s_e2.h, be2 + NDB + db - 1)
                        for k in range(NDB):
                            t.matmul(ps_m[:, sl, 0:LC],
                                     Wz_sb[:, k, db * P:(db + 1) * P],
                                     xn[:, k, t0:t0 + LC],
                                     start=(k == 0), stop=(k == NDB - 1)
                                     ).then_inc(s_p2.h,
                                                1 if k == NDB - 1 else 0)
                s_p2.n = bp2 + NDB + 1 + NDB + NDB

                @blk.scalar
                def _(s):
                    s.wait_ge(s_d2.h, bd2 + 1)
                    for db in range(NDB):
                        sl = db % 2
                        s.wait_ge(s_p2.h, bp2 + db + 1)
                        s.activation(xproj[:, db, 3:3 + LC], ps_m[:, sl, 0:LC],
                                     AF.Identity).then_inc(s_ev.h, 1)
                    s.wait_ge(s_p2.h, bp2 + NDB + 1)
                    s.activation(dtun_bf[:], ps_p1[0:64, :], AF.Identity)
                    s.activation(B_bf[:], ps_p1[64:80, :], AF.Identity)
                    s.activation(C_bf[:], ps_p2[:], AF.Identity)
                    s.drain().then_inc(s_ev.h, 1)
                    for db in range(NDB):
                        sl = db % 2
                        lnb = (ch_bufs[:, 0, :] if db % 2 == 0
                               else w2_bufs[:, 0, :])
                        s.wait_ge(s_p2.h, bp2 + NDB + 1 + db + 1)
                        if db >= 2:
                            s.wait_ge(s_d2.h, bd2 + db)
                        s.activation(tdt[:, 0, :], ps_m[:, sl, 0:LC], AF.Identity,
                                     bias=bdt_sb[:, db:db + 1])
                        if fence:
                            s.drain()
                        s.activation(lnb, tdt[:, 0, :], AF.Abs)
                        if fence:
                            s.drain()
                        s.activation(lnb, lnb, AF.Exp, scale=-1.0)
                        if fence:
                            s.drain()
                        s.activation(lnb, lnb, AF.Ln, bias=1.0)
                        if fence:
                            s.drain()
                        s.activation(dt_sb[:, db, :], tdt[:, 0, :],
                                     AF.Relu).then_inc(s_e2.h, 1)
                    for db in range(NDB):
                        sl = db % 2
                        s.wait_ge(s_p2.h, bp2 + 2 * NDB + 1 + db + 1)
                        s.activation(zs_sb[:, db, :], ps_m[:, sl, 0:LC],
                                     AF.Silu).then_inc(s_e2.h, 1)
                    s.wait_ge(s_d2.h, bd2 + NDB + 2)
                    op = None
                    for db in range(NDB):
                        if fence:
                            s.drain()
                        op = s.activation(u_sb[:, db, :], u_sb[:, db, :],
                                          AF.Silu)
                    op.then_inc(s_e2.h, 1)
                s_ev.n = bev + NDB + 1
                s_e2.n = be2 + NDB + NDB + 1

                @blk.vector
                def _(v):
                    if c == 0 and hh == 0:
                        v.memset(xproj[:, :, 0:3], 0.0)
                    elif c == 0:
                        v.tensor_copy(xproj[:, :, 0:3], pad_save[:])
                    else:
                        v.tensor_copy(xproj[:, :, 0:3], xproj[:, :, LC:LC + 3])
                    v.drain().then_inc(s_d2.h, 1)
                    for db in range(NDB):
                        lnb = (ch_bufs[:, 0, :] if db % 2 == 0
                               else w2_bufs[:, 0, :])
                        v.wait_ge(s_e2.h, be2 + db + 1)
                        v.tensor_tensor(dt_sb[:, db, :], dt_sb[:, db, :],
                                        lnb, ALU.add).then_inc(s_d2.h, 1)
                    v.wait_ge(s_ev.h, bev + NDB)
                    op = None
                    for db in range(NDB):
                        v.tensor_scalar(u_sb[:, db, :], xproj[:, db, 0:LC],
                                        convw_sb[:, db, 0:1], None, ALU.mult)
                        for k in range(1, 4):
                            if fence:
                                v.drain()
                            op = v.scalar_tensor_tensor(
                                u_sb[:, db, :], xproj[:, db, k:k + LC],
                                convw_sb[:, db, k:k + 1], u_sb[:, db, :],
                                ALU.mult, ALU.add)
                    op.then_inc(s_d2.h, 1)
                s_d2.n = bd2 + 2 + NDB

            if tap and tap.startswith("m_") and pp == 0 and c == 0:
                with nc.Block() as blk:
                    @blk.sync
                    def _(sync):
                        n16 = 0
                        if tap == "m_xpj":
                            sync.dma_start(out=tap_h[:, :, 0:LC + 3],
                                           in_=xproj[:]).then_inc(s_x.h, 16)
                            n16 = 16
                        elif tap == "m_dt":
                            sync.dma_start(out=tap_h[:, :, 0:LC],
                                           in_=dt_sb[:]).then_inc(s_x.h, 16)
                            n16 = 16
                        elif tap == "m_u":
                            sync.dma_start(out=tap_h[:, :, 0:LC],
                                           in_=u_sb[:]).then_inc(s_x.h, 16)
                            n16 = 16
                        elif tap == "m_zs":
                            sync.dma_start(out=tap_h[:, :, 0:LC],
                                           in_=zs_sb[:]).then_inc(s_x.h, 16)
                            n16 = 16
                        sync.wait_ge(s_x.h, s_x.n + n16)
                    if tap in ("m_xpj", "m_dt", "m_u", "m_zs"):
                        s_x.n += 16

                    @blk.gpsimd
                    def _(gp):
                        if tap == "m_BC":
                            gp.dma_start(out=tap_h[0:16, 0, 0:LC],
                                         in_=B_bf[:]).then_inc(s_w.h, 16)
                            gp.dma_start(out=tap_h[0:16, 1, 0:LC],
                                         in_=C_bf[:]).then_inc(s_w.h, 16)
                            gp.wait_ge(s_w.h, s_w.n + 32)
                    if tap == "m_BC":
                        s_w.n += 32

            with nc.named_scope("mB_%d_%d" % (pp, c)), \
                    nc.Block() as blk, contextlib.ExitStack() as sc:
                pc = sc.enter_context
                ps_bc = pc(nc.psum_tensor("psB_%d_%d" % (pp, c), [P, 4, 512], F32))

                bp2, bev, bd2, be2, bpl = (s_p2.n, s_ev.n, s_d2.n, s_e2.n,
                                           s_pl.n)

                @blk.tensor
                def _(t):
                    for n in range(NST):
                        nsl = n % 2
                        if n >= 2:
                            t.wait_ge(s_e2.h, be2 + n - 1)
                        t.matmul(ps_bc[:, nsl * 2, 0:LC], pick16[:, n, :],
                                 B_bf[:], start=True, stop=True)
                        t.matmul(ps_bc[:, nsl * 2 + 1, 0:LC], pick16[:, n, :],
                                 C_bf[:], start=True, stop=True
                                 ).then_inc(s_p2.h, 1)
                s_p2.n = bp2 + NST

                @blk.scalar
                def _(s):
                    for n in range(NST):
                        nsl = n % 2
                        s.wait_ge(s_p2.h, bp2 + n + 1)
                        if n >= 2:
                            s.wait_ge(s_pl.h, bpl + (n - 2) * NDB + NDB)
                        s.activation(Bb_sb[:, nsl, :], ps_bc[:, nsl * 2, 0:LC],
                                     AF.Identity)
                        s.activation(Cb_sb[:, nsl, :],
                                     ps_bc[:, nsl * 2 + 1, 0:LC],
                                     AF.Identity).then_inc(s_e2.h, 1)
                        for db in range(NDB):
                            ii = n * NDB + db
                            sl = ii % 2
                            if ii >= 2:
                                s.wait_ge(s_d2.h, bd2 + (ii - 2) * 3 + 3)
                            s.activation(a_bufs[:, sl, :], dt_sb[:, db, :],
                                         AF.Exp, scale=A_sb[:, db, n:n + 1])
                            if fence:
                                s.drain()
                            s.activation(am1_bufs[:, sl, :], a_bufs[:, sl, :],
                                         AF.Identity,
                                         scale=invA_sb[:, db, n:n + 1],
                                         bias=ninvA_sb[:, db, n:n + 1]
                                         ).then_inc(s_ev.h, 1)
                s_ev.n = bev + NST * NDB
                s_e2.n = be2 + NST

                @blk.vector
                def _(v):
                    for n in range(NST):
                        nsl = n % 2
                        v.wait_ge(s_e2.h, be2 + n + 1)
                        for db in range(NDB):
                            ii = n * NDB + db
                            sl = ii % 2
                            v.wait_ge(s_ev.h, bev + ii + 1)
                            if ii >= 2:
                                v.wait_ge(s_pl.h, bpl + ii - 1)
                            v.tensor_tensor(w2_bufs[:, sl, :],
                                            u_sb[:, db, :],
                                            Bb_sb[:, nsl, :], ALU.mult)
                            if fence:
                                v.drain()
                            v.tensor_tensor(w2_bufs[:, sl, :],
                                            am1_bufs[:, sl, :],
                                            w2_bufs[:, sl, :], ALU.mult)
                            if fence:
                                v.drain()
                            v.tensor_tensor_scan(h_bufs[:, sl, :],
                                                 a_bufs[:, sl, :],
                                                 w2_bufs[:, sl, :],
                                                 h_state[:, db, n:n + 1],
                                                 ALU.mult, ALU.add
                                                 ).then_inc(s_d2.h, 3)
                s_d2.n = bd2 + NST * NDB * 3

                @blk.gpsimd
                def _(gp):
                    for n in range(NST):
                        nsl = n % 2
                        for db in range(NDB):
                            ii = n * NDB + db
                            sl = ii % 2
                            gp.wait_ge(s_d2.h, bd2 + ii * 3 + 3)
                            if fence:
                                gp.drain()
                            gp.tensor_tensor(ch_bufs[:, 0, :],
                                             h_bufs[:, sl, :],
                                             Cb_sb[:, nsl, :], ALU.mult)
                            if fence:
                                gp.drain()
                            gp.tensor_copy(h_state[:, db, n:n + 1],
                                           h_bufs[:, sl, LC - 1:LC])
                            if n == 0:
                                gp.tensor_copy(y_sb[:, db, :],
                                               ch_bufs[:, 0, :]
                                               ).then_inc(s_pl.h, 1)
                            else:
                                gp.tensor_tensor(y_sb[:, db, :],
                                                 y_sb[:, db, :],
                                                 ch_bufs[:, 0, :], ALU.add
                                                 ).then_inc(s_pl.h, 1)
                    gp.drain()
                s_pl.n = bpl + NST * NDB

            if tap == "m_y" and pp == 0 and c == 0:
                with nc.Block() as blk:
                    @blk.sync
                    def _(sync):
                        sync.dma_start(out=tap_h[:, :, 0:LC],
                                       in_=y_sb[:]).then_inc(s_x.h, 16)
                        sync.wait_ge(s_x.h, s_x.n + 16)
                    s_x.n += 16

            with nc.named_scope("mC_%d_%d" % (pp, c)), \
                    nc.Block() as blk, contextlib.ExitStack() as sc:
                pc = sc.enter_context
                ps_wo = pc(nc.psum_tensor("psC_%d_%d" % (pp, c), [P, 2, 512], F32))

                bw, bp2, bd2 = s_w.n, s_p2.n, s_d2.n

                @blk.vector
                def _(v):
                    op = None
                    for db in range(NDB):
                        if fence:
                            v.drain()
                        v.scalar_tensor_tensor(ch_bufs[:, 0, :], u_sb[:, db, :],
                                               dp_sb[:, db:db + 1],
                                               y_sb[:, db, :], ALU.mult,
                                               ALU.add)
                        if fence:
                            v.drain()
                        op = v.tensor_tensor(xn[:, db, t0:t0 + LC],
                                             ch_bufs[:, 0, :],
                                             zs_sb[:, db, :], ALU.mult)
                    op.then_inc(s_d2.h, 1)
                    for db in range(NDB):
                        sl = db % 2
                        v.wait_ge(s_p2.h, bp2 + db + 1)
                        v.tensor_tensor(xr[:, db, t0:t0 + LC],
                                        xr[:, db, t0:t0 + LC],
                                        ps_wo[:, sl, 0:LC], ALU.add
                                        ).then_inc(s_d2.h, 1)
                s_d2.n = bd2 + 1 + NDB

                @blk.tensor
                def _(t):
                    t.wait_ge(s_d2.h, bd2 + 1)
                    for db in range(NDB):
                        sl = db % 2
                        if db >= 2:
                            t.wait_ge(s_d2.h, bd2 + 1 + db - 1)
                        mm = None
                        for k in range(NDB):
                            mm = t.matmul(ps_wo[:, sl, 0:LC],
                                          Wo_sb[:, k, db * P:(db + 1) * P],
                                          xn[:, k, t0:t0 + LC],
                                          start=(k == 0), stop=(k == NDB - 1))
                        mm.then_inc(s_p2.h, 1)
                s_p2.n = bp2 + NDB


        def emit_store(b, hh):
            with nc.Block() as blk:
                bx = s_x.n

                @blk.sync
                def _(sync):
                    dst = out_h[b].rearrange("(db p) l -> p db l", p=P)[
                        :, :, hh * LP:(hh + 1) * LP]
                    sync.dma_start(out=dst, in_=xr[:]).then_inc(s_x.h, 16)
                    sync.wait_ge(s_x.h, bx + 16)
                s_x.n = bx + 16

        for pp in range(npass):
            b, hh = divmod(pp, NH)
            if hh == 0 and do_mamba:
                with nc.Block() as blk:
                    @blk.vector
                    def _(v):
                        v.memset(h_state[:], 0.0)
                        v.drain()

            def ffn_call(tag, xin_dma=None, first_call=False, split=False):
                with contextlib.ExitStack() as fs:
                    fc = fs.enter_context
                    sg = fc(nc.sbuf_tensor("sgw_%s" % tag, [P, NDB, 1408], BF16))
                    su = fc(nc.sbuf_tensor("suw_%s" % tag, [P, NDB, 1408], BF16))
                    sd = fc(nc.sbuf_tensor("sdw_%s" % tag, [P, NJ, DM], BF16))
                    inter = fc(nc.sbuf_tensor("int_%s" % tag, [P, NJ, LP], BF16))
                    wbs = fc(nc.sbuf_tensor("wbs_%s" % tag, [P, 7, LP], F32))

                    def prefetch(fc0):
                        with nc.Block() as pblk:
                            def _pf(q):
                                pg, pu, pd = ((gT_h, uT_h, dT_h) if fc0
                                              else (scrg_h, scru_h, scrd_h))
                                gsl = pg[0].rearrange(
                                    "(db p) i -> p db i", p=P)[:, :, 0:1408]
                                usl = pu[0].rearrange(
                                    "(db p) i -> p db i", p=P)[:, :, 0:1408]
                                dsl = pd[0].rearrange(
                                    "(jj p) dd -> p jj dd", p=P)[:, 0:NJ, :]
                                q.dma_start(out=sg[:], in_=gsl
                                            ).then_inc(s_w.h, 16)
                                q.dma_start(out=su[:], in_=usl
                                            ).then_inc(s_w.h, 16)
                                q.dma_start(out=sd[:], in_=dsl
                                            ).then_inc(s_w.h, 16)
                            if fc0:
                                @pblk.gpsimd
                                def _(gp):
                                    _pf(gp)
                            else:
                                @pblk.sync
                                def _(sync):
                                    _pf(sync)
                            s_w.n += 48

                    if split:
                        for hw in range(2):
                            fc0 = first_call and hw == 0
                            prefetch(fc0)
                            emit_ln("%s_h%d" % (tag, hw),
                                    xin_dma=(xin_dma if hw == 0 else None),
                                    tok0=hw * 512, ntok=512)
                            emit_router("%s_h%d" % (tag, hw), wbs,
                                        tok0=hw * 512, ntok=512)
                            emit_ffn("%s_h%d" % (tag, hw), wbs, sg, su, sd,
                                     inter, skip_first_load=True,
                                     first_call=fc0, tok0=hw * 512, ntok=512)
                        return
                    prefetch(first_call)
                    emit_ln(tag, xin_dma=xin_dma)
                    emit_router(tag, wbs)
                    if tap == "wbs" and tag.endswith("_0") and pp == 0:
                        with nc.Block() as blk2:
                            @blk2.sync
                            def _(sync):
                                for e in range(7):
                                    sync.dma_start(out=tap_h[:, e, :],
                                                   in_=wbs[:, e, :]
                                                   ).then_inc(s_x.h, 16)
                                sync.wait_ge(s_x.h, s_x.n + 112)
                            s_x.n += 112
                    emit_ffn(tag, wbs, sg, su, sd, inter,
                             skip_first_load=True, first_call=first_call)

            if not do_ffn1:
                with nc.named_scope("p%d_ln1" % pp):
                    emit_ln("%d_0" % pp, xin_dma=(b, hh))
            if tap and pp == 0:
                with nc.Block() as blk, \
                        nc.sbuf_tensor("tapf", [P, NDB, LP], F32) as tapf:
                    @blk.vector
                    def _(v):
                        for db in range(NDB):
                            v.tensor_copy(tapf[:, db, :], xn[:, db, :])
                        v.drain().then_inc(s_d2.h, 1)
                    s_d2.n += 1

                    @blk.sync
                    def _(sync):
                        sync.wait_ge(s_d2.h, s_d2.n)
                        sync.dma_start(out=tap_h, in_=tapf[:]
                                       ).then_inc(s_x.h, 16)
                        sync.wait_ge(s_x.h, s_x.n + 16)
                    s_x.n += 16
            if do_ffn1:
                with nc.named_scope("p%d_ffn1" % pp):
                    ffn_call("%d_0" % pp, xin_dma=(b, hh),
                             first_call=(pp == 0))
            if do_mamba:
                with nc.named_scope("p%d_ln2" % pp):
                    emit_ln("%d_1" % pp)
                with nc.named_scope("p%d_mamba" % pp):
                    emit_mamba(pp, hh)
            if do_ffn2:
                with nc.named_scope("p%d_ffn2" % pp):
                    ffn_call("%d_2" % pp, split=split2)
            with nc.named_scope("p%d_store" % pp):
                emit_store(b, hh)

    return nc


def _host_layout(inputs):
    f32 = np.float32
    x = np.ascontiguousarray(
        np.asarray(inputs["L0_attn_x"], dtype=f32).transpose(0, 2, 1))
    wxT = np.ascontiguousarray(np.asarray(inputs["L0_attn_Wx"], f32).T)
    wzT = np.ascontiguousarray(np.asarray(inputs["L0_attn_Wz"], f32).T)
    woT = np.ascontiguousarray(np.asarray(inputs["L0_attn_Wo"], f32).T)
    wpT = np.ascontiguousarray(np.asarray(inputs["L0_attn_Wp"], f32).T)
    wdtT = np.ascontiguousarray(np.asarray(inputs["L0_attn_Wdt"], f32).T)
    bdt = np.ascontiguousarray(
        np.asarray(inputs["L0_attn_bdt"], f32).reshape(NDB, P).T)
    alog = np.ascontiguousarray(
        np.asarray(inputs["L0_attn_A_log"], f32).reshape(NDB, P, NST)
        .transpose(1, 0, 2))
    dp = np.ascontiguousarray(
        np.asarray(inputs["L0_attn_Dp"], f32).reshape(NDB, P).T)
    convw = np.ascontiguousarray(
        np.asarray(inputs["L0_attn_conv_w"], f32)[:, 0, :]
        .reshape(NDB, P, 4).transpose(1, 0, 2))
    sgT = np.asarray(inputs["L0_ffn_sg"], f32).T
    suT = np.asarray(inputs["L0_ffn_su"], f32).T
    sdT = np.asarray(inputs["L0_ffn_sd"], f32).T
    rgT = np.asarray(inputs["L0_ffn_rg"], f32).transpose(0, 2, 1)
    ruT = np.asarray(inputs["L0_ffn_ru"], f32).transpose(0, 2, 1)
    rdT = np.asarray(inputs["L0_ffn_rd"], f32).transpose(0, 2, 1)
    gT = np.ascontiguousarray(np.concatenate([sgT[None], rgT], axis=0))
    uT = np.ascontiguousarray(np.concatenate([suT[None], ruT], axis=0))
    dT = np.ascontiguousarray(np.concatenate([sdT[None], rdT], axis=0))
    wrT = np.ascontiguousarray(np.asarray(inputs["L0_ffn_wr"], f32).T)
    rb = np.ascontiguousarray(
        np.asarray(inputs["L0_ffn_rb"], f32).reshape(7, 1))
    shared = dict(wxT=wxT, wzT=wzT, woT=woT, wpT=wpT, wdtT=wdtT, bdt=bdt,
                  alog=alog, dp=dp, convw=convw, gT=gT, uT=uT, dT=dT,
                  wrT=wrT, rb=rb)
    return x, shared


def kernel(**inputs):
    x, shared = _host_layout(inputs)
    nc = build()
    in_maps = []
    for c in range(8):
        m = dict(shared)
        m["x"] = np.ascontiguousarray(x[2 * c:2 * c + 2])
        in_maps.append(m)
    res = run_bass_kernel_spmd(nc, in_maps, core_ids=list(range(8)))
    outs = [res.results[c]["out"] for c in range(8)]
    full = np.concatenate(outs, axis=0)
    return np.ascontiguousarray(full.transpose(0, 2, 1)).astype(np.float32)


# version 30 (final)
# speedup vs baseline: 1.1095x; 1.0839x over previous
import contextlib
import numpy as np

import concourse.bass as bass
import concourse.mybir as mybir
from concourse.bass_utils import run_bass_kernel_spmd

F32 = mybir.dt.float32
BF16 = mybir.dt.bfloat16
AF = mybir.ActivationFunctionType
ALU = mybir.AluOpType
AXX = mybir.AxisListType.X

P = 128
NDB = 8
DM = 1024
LSEQ = 2048
NB = 2
NH = 2
LP = 1024
LC = 512
NCH = 2
II = 2816
NJ = 11
NIC = 2
NE = 8
NST = 16
RDT = 64
EPS = 1e-5
AEPS = 1e-8


class Ctr:
    __slots__ = ("h", "n")

    def __init__(self, h):
        self.h = h
        self.n = 0


def build(cfg=None):
    cfg = dict(cfg or {})
    npass = cfg.get("npass", NB * NH)
    do_ffn1 = cfg.get("ffn1", True)
    do_mamba = cfg.get("mamba", True)
    do_ffn2 = cfg.get("ffn2", True)
    n_exp = cfg.get("ne", NE)
    split2 = cfg.get("split2", False)
    fence = cfg.get("fence", False)

    tap = cfg.get("tap")
    nc = bass.Bass("TRN2", target_bir_lowering=False)

    x_h = nc.dram_tensor("x", [NB, DM, LSEQ], F32, kind="ExternalInput").ap()
    wxT_h = nc.dram_tensor("wxT", [DM, DM], F32, kind="ExternalInput").ap()
    wzT_h = nc.dram_tensor("wzT", [DM, DM], F32, kind="ExternalInput").ap()
    woT_h = nc.dram_tensor("woT", [DM, DM], F32, kind="ExternalInput").ap()
    wpT_h = nc.dram_tensor("wpT", [DM, 96], F32, kind="ExternalInput").ap()
    wdtT_h = nc.dram_tensor("wdtT", [RDT, DM], F32, kind="ExternalInput").ap()
    bdt_h = nc.dram_tensor("bdt", [P, NDB], F32, kind="ExternalInput").ap()
    alog_h = nc.dram_tensor("alog", [P, NDB, NST], F32, kind="ExternalInput").ap()
    dp_h = nc.dram_tensor("dp", [P, NDB], F32, kind="ExternalInput").ap()
    convw_h = nc.dram_tensor("convw", [P, NDB, 4], F32, kind="ExternalInput").ap()
    gT_h = nc.dram_tensor("gT", [NE, DM, II], F32, kind="ExternalInput").ap()
    uT_h = nc.dram_tensor("uT", [NE, DM, II], F32, kind="ExternalInput").ap()
    dT_h = nc.dram_tensor("dT", [NE, II, DM], F32, kind="ExternalInput").ap()
    wrT_h = nc.dram_tensor("wrT", [DM, 7], F32, kind="ExternalInput").ap()
    rb_h = nc.dram_tensor("rb", [7, 1], F32, kind="ExternalInput").ap()
    out_h = nc.dram_tensor("out", [NB, DM, LSEQ], F32, kind="ExternalOutput").ap()
    scrg_h = nc.dram_tensor("scrg", [NE, DM, II], BF16, kind="Internal").ap()
    scru_h = nc.dram_tensor("scru", [NE, DM, II], BF16, kind="Internal").ap()
    scrd_h = nc.dram_tensor("scrd", [NE, II, DM], BF16, kind="Internal").ap()
    tap_h = None
    if tap:
        tap_h = nc.dram_tensor("tap", [P, NDB, LP], F32,
                               kind="ExternalOutput").ap()

    es = contextlib.ExitStack()
    with es:
        ec = es.enter_context

        xr = ec(nc.sbuf_tensor("xr", [P, NDB, LP], F32))
        xn = ec(nc.sbuf_tensor("xn", [P, NDB, LP], BF16))
        identf = ec(nc.sbuf_tensor("identf", [P, P], F32))
        pick16 = ec(nc.sbuf_tensor("pick16", [16, 16, P], BF16))
        pickf = ec(nc.sbuf_tensor("pickf", [16, 7, P], F32))
        ones_col = ec(nc.sbuf_tensor("ones_col", [P, 1], F32))
        ones_bf = ec(nc.sbuf_tensor("ones_bf", [P, 1], BF16))
        ones_row = ec(nc.sbuf_tensor("ones_row", [1, P], F32))
        bdt_sb = ec(nc.sbuf_tensor("bdt_sb", [P, NDB], F32))
        dp_sb = ec(nc.sbuf_tensor("dp_sb", [P, NDB], F32))
        convw_sb = ec(nc.sbuf_tensor("convw_sb", [P, NDB, 4], F32))
        alog_sb = ec(nc.sbuf_tensor("alog_sb", [P, NDB, NST], F32))
        A_sb = ec(nc.sbuf_tensor("A_sb", [P, NDB, NST], F32))
        invA_sb = ec(nc.sbuf_tensor("invA_sb", [P, NDB, NST], F32))
        ninvA_sb = ec(nc.sbuf_tensor("ninvA_sb", [P, NDB, NST], F32))
        wrT_sb = ec(nc.sbuf_tensor("wrT_sb", [P, NDB, 7], BF16))
        rb_sb = ec(nc.sbuf_tensor("rb_sb", [7, 1], F32))
        wdtT_sb = ec(nc.sbuf_tensor("wdtT_sb", [RDT, DM], BF16))
        wpT_sb = ec(nc.sbuf_tensor("wpT_sb", [P, NDB, 96], BF16))
        h_state = ec(nc.sbuf_tensor("h_state", [P, NDB, NST], F32))
        pad_save = ec(nc.sbuf_tensor("pad_save", [P, NDB, 3], BF16))
        cst = ec(nc.sbuf_tensor("cst", [P, 4], F32))
        _cvals = [EPS, 1.0, -1.0, -0.5]
        for _i, _v in enumerate(_cvals):
            nc.const_aps.register(F32, _v, cst[:, _i:_i + 1])

        s_x = Ctr(ec(nc.semaphore("s_x")))
        s_w = Ctr(ec(nc.semaphore("s_w")))
        s_ps = Ctr(ec(nc.semaphore("s_ps")))
        s_in = Ctr(ec(nc.semaphore("s_in")))
        s_ev = Ctr(ec(nc.semaphore("s_ev")))
        s_dn = Ctr(ec(nc.semaphore("s_dn")))
        s_rs = Ctr(ec(nc.semaphore("s_rs")))
        s_p2 = Ctr(ec(nc.semaphore("s_p2")))
        s_d2 = Ctr(ec(nc.semaphore("s_d2")))
        s_e2 = Ctr(ec(nc.semaphore("s_e2")))
        s_pl = Ctr(ec(nc.semaphore("s_pl")))

        with nc.Block() as blk:
            @blk.sync
            def _(sync):
                sync.dma_start(out=bdt_sb[:], in_=bdt_h).then_inc(s_x.h, 16)
                sync.dma_start(out=dp_sb[:], in_=dp_h).then_inc(s_x.h, 16)
                sync.dma_start(out=convw_sb[:], in_=convw_h).then_inc(s_x.h, 16)
                sync.dma_start(out=alog_sb[:], in_=alog_h).then_inc(s_x.h, 16)
                sync.dma_start(out=rb_sb[:], in_=rb_h).then_inc(s_x.h, 16)
            s_x.n += 80

            @blk.gpsimd
            def _(gp):
                gp.dma_start(out=wrT_sb[:],
                             in_=wrT_h.rearrange("(db p) e -> p db e", p=P)
                             ).then_inc(s_w.h, 16)
                gp.dma_start(out=wdtT_sb[:], in_=wdtT_h).then_inc(s_w.h, 16)
                gp.dma_start(out=wpT_sb[:],
                             in_=wpT_h.rearrange("(db p) m -> p db m", p=P)
                             ).then_inc(s_w.h, 16)
            s_w.n += 48

        with nc.Block() as blk, \
                nc.sbuf_tensor("iota_a", [P, P], F32) as iota_a, \
                nc.sbuf_tensor("iota_b", [P, P], F32) as iota_b, \
                nc.sbuf_tensor("piota_a", [16, 16, P], F32) as piota_a, \
                nc.sbuf_tensor("piota_b", [16, 16, P], F32) as piota_b:
            @blk.gpsimd
            def _(gp):
                gp.iota(iota_a[:], [[0, P]], channel_multiplier=1,
                        allow_small_or_imprecise_dtypes=True)
                gp.iota(iota_b[:], [[1, P]], channel_multiplier=0,
                        allow_small_or_imprecise_dtypes=True)
                gp.iota(piota_a[:], [[0, 16], [0, P]], channel_multiplier=1,
                        allow_small_or_imprecise_dtypes=True)
                gp.iota(piota_b[:], [[1, 16], [0, P]], channel_multiplier=0,
                        allow_small_or_imprecise_dtypes=True
                        ).then_inc(s_pl.h, 1)
            s_pl.n += 1

            @blk.vector
            def _(v):
                v.wait_ge(s_pl.h, s_pl.n)
                v.tensor_tensor(identf[:], iota_a[:], iota_b[:], ALU.is_equal)
                v.tensor_tensor(pick16[:], piota_a[:], piota_b[:], ALU.is_equal)
                v.tensor_tensor(pickf[:], piota_a[:, 0:7, :],
                                piota_b[:, 0:7, :], ALU.is_equal)
                v.memset(ones_col[:], 1.0)
                v.memset(ones_bf[:], 1.0)
                v.memset(ones_row[:], 1.0)
                v.memset(h_state[:], 0.0)
                v.memset(pad_save[:], 0.0)
                for _i, _v in enumerate(_cvals):
                    v.memset(cst[:, _i:_i + 1], _v)
                v.drain()

            @blk.scalar
            def _(s):
                s.wait_ge(s_x.h, s_x.n)
                s.activation(A_sb[:], alog_sb[:], AF.Exp).then_inc(s_ev.h, 1)
            s_ev.n += 1

        with nc.Block() as blk:
            for _reg in (blk.tensor, blk.vector, blk.scalar, blk.gpsimd,
                         blk.sync):
                def _body(eng, _x=s_x.n, _w=s_w.n):
                    eng.wait_ge(s_x.h, _x)
                    eng.wait_ge(s_w.h, _w)
                _reg(_body)

        with nc.Block() as blk:
            @blk.vector
            def _(v):
                v.wait_ge(s_ev.h, s_ev.n)
                v.tensor_single_scalar(A_sb[:], A_sb[:], -1.0, ALU.mult)
                v.drain()
                v.tensor_single_scalar(invA_sb[:], A_sb[:], AEPS, ALU.add)
                v.drain()
                v.reciprocal(invA_sb[:], invA_sb[:])
                v.drain()
                v.tensor_single_scalar(ninvA_sb[:], invA_sb[:], -1.0, ALU.mult)
                v.drain()

        def emit_ln(tag, xin_dma=None, tok0=0, ntok=LP):
            NLCW = ntok // 512
            with nc.Block() as blk, contextlib.ExitStack() as sc:
                pc = sc.enter_context
                ps_s1 = pc(nc.psum_tensor("ps_s1_%s" % tag, [1, ntok], F32))
                ps_s2 = pc(nc.psum_tensor("ps_s2_%s" % tag, [1, ntok], F32))
                ps_b1 = pc(nc.psum_tensor("ps_b1_%s" % tag, [P, 2, 512], F32))
                xsq = pc(nc.sbuf_tensor("xsq_%s" % tag, [P, 2, 512], BF16))
                stat = pc(nc.sbuf_tensor("stat_%s" % tag, [1, 4, ntok], F32))
                tmpf = pc(nc.sbuf_tensor("tmpf_%s" % tag, [P, 512], F32))

                bev, bp2, bd2, be2 = s_ev.n, s_p2.n, s_d2.n, s_e2.n
                NSQ = NLCW * NDB

                def W(lc):
                    return slice(tok0 + lc * 512, tok0 + (lc + 1) * 512)

                if xin_dma is not None:
                    @blk.sync
                    def _(sync):
                        b, hh = xin_dma
                        srcx = x_h[b].rearrange("(db p) l -> p db l", p=P)[
                            :, :, hh * LP:(hh + 1) * LP]
                        sync.dma_start(out=xr[:], in_=srcx).then_inc(s_x.h, 16)
                    s_x.n += 16

                @blk.scalar
                def _(s):
                    s.wait_ge(s_x.h, s_x.n)
                    for db in range(NDB):
                        s.activation(xn[:, db, tok0:tok0 + ntok],
                                     xr[:, db, tok0:tok0 + ntok], AF.Identity
                                     ).then_inc(s_e2.h, 1)
                    k = 0
                    for lc in range(NLCW):
                        for db in range(NDB):
                            if k >= 2:
                                s.wait_ge(s_p2.h, bp2 + k - 1)
                            s.activation(xsq[:, k % 2, :], xr[:, db, W(lc)],
                                         AF.Square).then_inc(s_ev.h, 1)
                            k += 1
                    s.wait_ge(s_d2.h, bd2 + 1)
                    s.activation(stat[:, 1, :], stat[:, 1, :], AF.Ln, bias=EPS)
                    s.drain()
                    s.activation(stat[:, 2, :], stat[:, 1, :], AF.Exp,
                                 scale=-0.5)
                    s.drain().then_inc(s_e2.h, 1)
                s_ev.n = bev + NSQ
                s_e2.n = be2 + NDB + 1

                @blk.tensor
                def _(t):
                    for lc in range(NLCW):
                        for db in range(NDB):
                            if lc == 0:
                                t.wait_ge(s_e2.h, be2 + db + 1)
                            t.matmul(ps_s1[0:1, lc * 512:(lc + 1) * 512],
                                     ones_bf[:], xn[:, db, W(lc)],
                                     start=(db == 0), stop=(db == NDB - 1))
                    k = 0
                    for lc in range(NLCW):
                        for db in range(NDB):
                            t.wait_ge(s_ev.h, bev + k + 1)
                            t.matmul(ps_s2[0:1, lc * 512:(lc + 1) * 512],
                                     ones_bf[:], xsq[:, k % 2, :],
                                     start=(db == 0), stop=(db == NDB - 1)
                                     ).then_inc(s_p2.h, 1)
                            k += 1
                    t.wait_ge(s_d2.h, bd2 + 2)
                    for lc in range(NLCW):
                        if lc >= 1:
                            t.wait_ge(s_d2.h, bd2 + 2 + lc)
                        t.matmul(ps_b1[:, 0, :], ones_row[:],
                                 stat[0:1, 2, lc * 512:(lc + 1) * 512],
                                 start=True, stop=True)
                        t.matmul(ps_b1[:, 1, :], ones_row[:],
                                 stat[0:1, 3, lc * 512:(lc + 1) * 512],
                                 start=True, stop=True).then_inc(s_p2.h, 1)
                s_p2.n = bp2 + NSQ + NLCW

                @blk.vector
                def _(v):
                    v.wait_ge(s_p2.h, bp2 + NSQ)
                    v.tensor_single_scalar(stat[:, 0, :], ps_s1[0:1, :],
                                           1.0 / DM, ALU.mult)
                    v.drain()
                    v.tensor_tensor(stat[:, 1, :], stat[:, 0, :],
                                    stat[:, 0, :], ALU.mult)
                    v.drain()
                    v.scalar_tensor_tensor(stat[:, 1, :], ps_s2[0:1, :],
                                           1.0 / DM, stat[:, 1, :], ALU.mult,
                                           ALU.subtract)
                    v.drain().then_inc(s_d2.h, 1)
                    v.wait_ge(s_e2.h, be2 + NDB + 1)
                    v.scalar_tensor_tensor(stat[:, 3, :], stat[:, 0, :], -1.0,
                                           stat[:, 2, :], ALU.mult, ALU.mult)
                    v.drain().then_inc(s_d2.h, 1)
                    for lc in range(NLCW):
                        v.wait_ge(s_p2.h, bp2 + NSQ + lc + 1)
                        op = None
                        for db in range(NDB):
                            if fence:
                                v.drain()
                            v.tensor_tensor(tmpf[:], xr[:, db, W(lc)],
                                            ps_b1[:, 0, :], ALU.mult)
                            if fence:
                                v.drain()
                            op = v.tensor_tensor(xn[:, db, W(lc)],
                                                 tmpf[:], ps_b1[:, 1, :],
                                                 ALU.add)
                        op.then_inc(s_d2.h, 1)
                s_d2.n = bd2 + 2 + NLCW

        def emit_router(tag, wbs, tok0=0, ntok=LP):
            NLCW = ntok // 512
            NTT = ntok // P
            with contextlib.ExitStack() as sc:
                pc = sc.enter_context
                ps_r = pc(nc.psum_tensor("ps_r_%s" % tag, [7, 2, 512], F32))
                lt_ps = pc(nc.psum_tensor("lt_ps_%s" % tag, [P, NTT, 7], F32))
                wcm_ps = pc(nc.psum_tensor("wcm_ps_%s" % tag, [7, ntok], F32))
                wb_ps = pc(nc.psum_tensor("wb_ps_%s" % tag, [P, 2, 512], F32))
                lg_cm = pc(nc.sbuf_tensor("lg_cm_%s" % tag, [7, ntok], F32))
                lg_tm = pc(nc.sbuf_tensor("lg_tm_%s" % tag, [P, NTT, 7], F32))
                e_tm = pc(nc.sbuf_tensor("e_tm_%s" % tag, [P, NTT, 7], F32))
                w_tm = pc(nc.sbuf_tensor("w_tm_%s" % tag, [P, NTT, 7], F32))
                mstat = pc(nc.sbuf_tensor("mstat_%s" % tag, [P, 5, NTT], F32))
                mk = pc(nc.sbuf_tensor("mk_%s" % tag, [P, NTT, 7], F32))
                em = pc(nc.sbuf_tensor("em_%s" % tag, [P, NTT, 7], F32))
                w_cm = pc(nc.sbuf_tensor("w_cm_%s" % tag, [7, ntok], F32))

                bev, bp2, bd2, be2 = s_ev.n, s_p2.n, s_d2.n, s_e2.n

                def W(lc):
                    return slice(tok0 + lc * 512, tok0 + (lc + 1) * 512)

                with nc.Block() as blk:
                    @blk.tensor
                    def _(t):
                        for lc in range(NLCW):
                            for db in range(NDB):
                                t.matmul(ps_r[:, lc % 2, :], wrT_sb[:, db, :],
                                         xn[:, db, W(lc)],
                                         start=(db == 0), stop=(db == NDB - 1)
                                         ).then_inc(
                                             s_p2.h, 1 if db == NDB - 1 else 0)
                        t.wait_ge(s_ev.h, bev + NLCW)
                        for tt in range(NTT):
                            t.transpose(lt_ps[:, tt, :],
                                        lg_cm[0:7, tt * P:(tt + 1) * P],
                                        identf[0:7, 0:7]).then_inc(
                                            s_p2.h, 1 if tt == NTT - 1 else 0)
                        t.wait_ge(s_d2.h, bd2 + 2)
                        for tt in range(NTT):
                            t.transpose(wcm_ps[:, tt * P:(tt + 1) * P],
                                        w_tm[:, tt, :], identf[:]).then_inc(
                                            s_p2.h, 1 if tt == NTT - 1 else 0)
                        t.wait_ge(s_e2.h, be2 + 3)
                        k = 0
                        for e in range(7):
                            for lc in range(NLCW):
                                if k >= 2:
                                    t.wait_ge(s_ev.h, bev + NLCW + k - 1)
                                t.matmul(wb_ps[:, k % 2, :], pickf[0:7, e, :],
                                         w_cm[:, lc * 512:(lc + 1) * 512],
                                         start=True, stop=True
                                         ).then_inc(s_p2.h, 1)
                                k += 1
                    s_p2.n = bp2 + NLCW + 2 + 7 * NLCW

                    @blk.scalar
                    def _(s):
                        for lc in range(NLCW):
                            s.wait_ge(s_p2.h, bp2 + lc + 1)
                            s.activation(lg_cm[0:7, lc * 512:(lc + 1) * 512],
                                         ps_r[:, lc % 2, :], AF.Identity,
                                         bias=rb_sb[:, 0:1]).then_inc(s_ev.h, 1)
                        s.wait_ge(s_p2.h, bp2 + NLCW + 1)
                        s.activation(lg_tm[:], lt_ps[:], AF.Identity
                                     ).then_inc(s_e2.h, 1)
                        s.wait_ge(s_d2.h, bd2 + 1)
                        op = None
                        for tt in range(NTT):
                            op = s.activation(e_tm[:, tt, :], lg_tm[:, tt, :],
                                              AF.Exp,
                                              bias=mstat[:, 1, tt:tt + 1])
                        op.then_inc(s_e2.h, 1)
                        s.wait_ge(s_p2.h, bp2 + NLCW + 2)
                        s.activation(w_cm[:], wcm_ps[0:7, :], AF.Identity
                                     ).then_inc(s_e2.h, 1)
                        k = 0
                        for e in range(7):
                            for lc in range(NLCW):
                                s.wait_ge(s_p2.h, bp2 + NLCW + 2 + k + 1)
                                s.activation(wbs[:, e, W(lc)],
                                             wb_ps[:, k % 2, :], AF.Identity
                                             ).then_inc(s_ev.h, 1)
                                k += 1
                    s_ev.n = bev + NLCW + 7 * NLCW
                    s_e2.n = be2 + 3

                    @blk.vector
                    def _(v):
                        v.wait_ge(s_e2.h, be2 + 1)
                        for tt in range(NTT):
                            v.tensor_reduce(mstat[:, 0, tt:tt + 1],
                                            lg_tm[:, tt, :], AXX, ALU.max)
                        v.drain()
                        v.tensor_single_scalar(mstat[:, 1, :], mstat[:, 0, :],
                                               -1.0, ALU.mult)
                        v.drain().then_inc(s_d2.h, 1)
                        v.wait_ge(s_e2.h, be2 + 2)
                        for tt in range(NTT):
                            v.tensor_reduce(mstat[:, 2, tt:tt + 1],
                                            e_tm[:, tt, :], AXX, ALU.max)
                        v.drain()
                        for tt in range(NTT):
                            v.tensor_scalar(mk[:, tt, :], e_tm[:, tt, :],
                                            mstat[:, 2, tt:tt + 1], None,
                                            ALU.is_lt)
                        v.drain()
                        for tt in range(NTT):
                            v.tensor_tensor(em[:, tt, :], e_tm[:, tt, :],
                                            mk[:, tt, :], ALU.mult)
                        v.drain()
                        for tt in range(NTT):
                            v.tensor_reduce(mstat[:, 3, tt:tt + 1],
                                            em[:, tt, :], AXX, ALU.max)
                        v.drain()
                        v.tensor_tensor(mstat[:, 4, :], mstat[:, 2, :],
                                        mstat[:, 3, :], ALU.add)
                        v.drain()
                        v.reciprocal(mstat[:, 4, :], mstat[:, 4, :])
                        v.drain()
                        for tt in range(NTT):
                            v.tensor_scalar(mk[:, tt, :], e_tm[:, tt, :],
                                            mstat[:, 3, tt:tt + 1], None,
                                            ALU.is_ge)
                        v.drain()
                        for tt in range(NTT):
                            v.tensor_tensor(em[:, tt, :], e_tm[:, tt, :],
                                            mk[:, tt, :], ALU.mult)
                        v.drain()
                        for tt in range(NTT):
                            v.tensor_scalar(w_tm[:, tt, :], em[:, tt, :],
                                            mstat[:, 4, tt:tt + 1], 0.5,
                                            ALU.mult, ALU.mult)
                        v.drain().then_inc(s_d2.h, 1)
                    s_d2.n = bd2 + 2

        def emit_ffn(tag, wbs, sg, su, sd, inter, skip_first_load=False,
                     first_call=False, plan=((0, 11), (11, 11)),
                     tok0=0, ntok=LP):
            with nc.Block() as blk, contextlib.ExitStack() as sc:
                pc = sc.enter_context
                ps_g = pc(nc.psum_tensor("ps_g_%s" % tag, [P, 2, 512], F32))
                ps_u = pc(nc.psum_tensor("ps_u_%s" % tag, [P, 2, 512], F32))
                ps_o = pc(nc.psum_tensor("ps_o_%s" % tag, [P, 2, LP], F32))
                sgact = pc(nc.sbuf_tensor("sgact_%s" % tag, [P, 2, 512], F32))
                tmp = pc(nc.sbuf_tensor("tmpr_%s" % tag, [P, 2, LP], F32))

                bw, bps, bin_, bev, bdn, brs = (s_w.n, s_ps.n, s_in.n,
                                                s_ev.n, s_dn.n, s_rs.n)
                if skip_first_load:
                    bw = s_w.n - 48
                NICP = len(plan)
                NIT = n_exp * NICP
                NLCW = ntok // 512
                jbase = [0]
                for _, jc in plan:
                    jbase.append(jbase[-1] + jc)
                npair = [NLCW * b for b in jbase]
                PPE = npair[-1]

                def WL(lc):
                    return slice(tok0 + lc * 512, tok0 + (lc + 1) * 512)

                def pair0(it):
                    e, ic = divmod(it, NICP)
                    return e * PPE + npair[ic]

                def dcnt(it):
                    return it * NDB

                bx0 = s_x.n

                def _loads(q):
                    gsrc, usrc, dsrc = ((gT_h, uT_h, dT_h) if first_call else
                                        (scrg_h, scru_h, scrd_h))
                    for it in range(1 if skip_first_load else 0, NIT):
                        e, ic = divmod(it, NICP)
                        js, jc = plan[ic]
                        gsl = gsrc[e].rearrange("(db p) i -> p db i", p=P)[
                            :, :, js * P:(js + jc) * P]
                        usl = usrc[e].rearrange("(db p) i -> p db i", p=P)[
                            :, :, js * P:(js + jc) * P]
                        dsl = dsrc[e].rearrange("(jj p) dd -> p jj dd",
                                                p=P)[:, js:js + jc, :]
                        if it > 0:
                            q.wait_ge(s_ps.h, bps + pair0(it))
                            if first_call:
                                q.wait_ge(s_x.h, bx0 + it * 48)
                        q.dma_start(out=sg[:, :, 0:jc * P], in_=gsl
                                    ).then_inc(s_w.h, 16)
                        q.dma_start(out=su[:, :, 0:jc * P], in_=usl
                                    ).then_inc(s_w.h, 16)
                        if it > 0:
                            q.wait_ge(s_dn.h, bdn + dcnt(it))
                            if first_call:
                                q.wait_ge(s_x.h, bx0 + it * 48)
                        q.dma_start(out=sd[:, 0:jc, :], in_=dsl
                                    ).then_inc(s_w.h, 16)

                if first_call:
                    @blk.gpsimd
                    def _(gp):
                        _loads(gp)
                else:
                    @blk.sync
                    def _(sync):
                        _loads(sync)
                s_w.n = bw + NIT * 48

                if first_call:
                    @blk.sync
                    def _(sync):
                        for it in range(NIT):
                            e, ic = divmod(it, NICP)
                            js, jc = plan[ic]
                            gds = scrg_h[e].rearrange("(db p) i -> p db i", p=P)[
                                :, :, js * P:(js + jc) * P]
                            uds = scru_h[e].rearrange("(db p) i -> p db i", p=P)[
                                :, :, js * P:(js + jc) * P]
                            dds = scrd_h[e].rearrange(
                                "(jj p) dd -> p jj dd", p=P)[:, js:js + jc, :]
                            gu_rdy = 48 if it == 0 else 32
                            sync.wait_ge(s_w.h, bw + it * 48 + gu_rdy)
                            sync.dma_start(out=gds, in_=sg[:, :, 0:jc * P]
                                           ).then_inc(s_x.h, 16)
                            sync.dma_start(out=uds, in_=su[:, :, 0:jc * P]
                                           ).then_inc(s_x.h, 16)
                            sync.wait_ge(s_w.h, bw + it * 48 + 48)
                            sync.dma_start(out=dds, in_=sd[:, 0:jc, :]
                                           ).then_inc(s_x.h, 16)
                        sync.wait_ge(s_x.h, bx0 + NIT * 48)
                    s_x.n = bx0 + NIT * 48

                @blk.tensor
                def _(t):
                    for it in range(NIT):
                        e, ic = divmod(it, NICP)
                        js, jc = plan[ic]
                        gu_ready = 48 if (it == 0 and skip_first_load) else 32
                        t.wait_ge(s_w.h, bw + it * 48 + gu_ready)
                        for j in range(jc):
                            for lc in range(NLCW):
                                n = pair0(it) + j * NLCW + lc
                                sl = n % 2
                                if n >= 2:
                                    t.wait_ge(s_in.h, bin_ + n - 1)
                                for k in range(NDB):
                                    t.matmul(ps_g[:, sl, :],
                                             sg[:, k, j * P:(j + 1) * P],
                                             xn[:, k, WL(lc)],
                                             start=(k == 0),
                                             stop=(k == NDB - 1))
                                mm = None
                                for k in range(NDB):
                                    mm = t.matmul(
                                        ps_u[:, sl, :],
                                        su[:, k, j * P:(j + 1) * P],
                                        xn[:, k, WL(lc)],
                                        start=(k == 0), stop=(k == NDB - 1))
                                mm.then_inc(s_ps.h, 1)
                        t.wait_ge(s_w.h, bw + it * 48 + 48)
                        for db in range(NDB):
                            gdb = it * NDB + db
                            sl = db % 2
                            if gdb >= 2:
                                t.wait_ge(s_rs.h, brs + gdb - 1)
                            mm = None
                            for j in range(jc):
                                if db == 0:
                                    t.wait_ge(s_in.h,
                                              bin_ + pair0(it) + (j + 1) * NLCW)
                                for lc in range(NLCW):
                                    mm = t.matmul(
                                        ps_o[:, sl, lc * 512:(lc + 1) * 512],
                                        sd[:, j, db * P:(db + 1) * P],
                                        inter[:, j, lc * 512:(lc + 1) * 512],
                                        start=(j == 0), stop=(j == jc - 1))
                            mm.then_inc(s_dn.h, 1)
                s_ps.n = bps + n_exp * PPE
                s_dn.n = bdn + NIT * NDB

                @blk.scalar
                def _(s):
                    for it in range(NIT):
                        e, ic = divmod(it, NICP)
                        js, jc = plan[ic]
                        for j in range(jc):
                            for lc in range(NLCW):
                                n = pair0(it) + j * NLCW + lc
                                sl = n % 2
                                s.wait_ge(s_ps.h, bps + n + 1)
                                if n >= 2:
                                    s.wait_ge(s_in.h, bin_ + n - 1)
                                s.activation(sgact[:, sl, :], ps_g[:, sl, :],
                                             AF.Silu).then_inc(s_ev.h, 1)
                s_ev.n = bev + n_exp * PPE

                @blk.vector
                def _(v):
                    for it in range(NIT):
                        e, ic = divmod(it, NICP)
                        js, jc = plan[ic]
                        for j in range(jc):
                            for lc in range(NLCW):
                                n = pair0(it) + j * NLCW + lc
                                sl = n % 2
                                v.wait_ge(s_ev.h, bev + n + 1)
                                v.tensor_tensor(
                                    inter[:, j, lc * 512:(lc + 1) * 512],
                                    sgact[:, sl, :], ps_u[:, sl, :],
                                    ALU.mult).then_inc(s_in.h, 1)
                        for db in range(NDB):
                            gdb = it * NDB + db
                            sl = db % 2
                            v.wait_ge(s_dn.h, bdn + gdb + 1)
                            xw = xr[:, db, tok0:tok0 + ntok]
                            if e == 0:
                                op = v.scalar_tensor_tensor(
                                    xw, ps_o[:, sl, 0:ntok], 0.5,
                                    xw, ALU.mult, ALU.add)
                            else:
                                if fence:
                                    v.drain()
                                v.scalar_tensor_tensor(
                                    tmp[:, sl, 0:ntok], ps_o[:, sl, 0:ntok],
                                    1.0, wbs[:, e - 1, tok0:tok0 + ntok],
                                    ALU.bypass, ALU.mult)
                                if fence:
                                    v.drain()
                                op = v.tensor_tensor(xw, xw,
                                                     tmp[:, sl, 0:ntok],
                                                     ALU.add)
                            op.then_inc(s_rs.h, 1)
                s_in.n = bin_ + n_exp * PPE
                s_rs.n = brs + NIT * NDB

        def emit_mamba(pp, hh):
            with contextlib.ExitStack() as mw:
                mc = mw.enter_context
                W1 = mc(nc.sbuf_tensor("W1_%d" % pp, [P, NDB, DM], BF16))
                Wz_sb = mc(nc.sbuf_tensor("Wzs_%d" % pp, [P, NDB, DM], BF16))
                Wo_sb = mc(nc.sbuf_tensor("Wos_%d" % pp, [P, NDB, DM], BF16))
                xproj = mc(nc.sbuf_tensor("xpj_%d" % pp, [P, NDB, LC + 3], BF16))
                dt_sb = mc(nc.sbuf_tensor("dt_%d" % pp, [P, NDB, LC], F32))
                u_sb = mc(nc.sbuf_tensor("u_%d" % pp, [P, NDB, LC], F32))
                zs_sb = mc(nc.sbuf_tensor("zs_%d" % pp, [P, NDB, LC], BF16))
                y_sb = mc(nc.sbuf_tensor("y_%d" % pp, [P, NDB, LC], F32))
                dtun_bf = mc(nc.sbuf_tensor("dtun_%d" % pp, [RDT, LC], BF16))
                B_bf = mc(nc.sbuf_tensor("Bbf_%d" % pp, [16, LC], BF16))
                C_bf = mc(nc.sbuf_tensor("Cbf_%d" % pp, [16, LC], BF16))
                Bb_sb = mc(nc.sbuf_tensor("Bb_%d" % pp, [P, 2, LC], F32))
                Cb_sb = mc(nc.sbuf_tensor("Cb_%d" % pp, [P, 2, LC], F32))
                a_bufs = mc(nc.sbuf_tensor("abuf_%d" % pp, [P, 2, LC], F32))
                am1_bufs = mc(nc.sbuf_tensor("am1_%d" % pp, [P, 2, LC], F32))
                w2_bufs = mc(nc.sbuf_tensor("w2b_%d" % pp, [P, 2, LC], F32))
                h_bufs = mc(nc.sbuf_tensor("hbuf_%d" % pp, [P, 2, LC], F32))
                ch_bufs = mc(nc.sbuf_tensor("chb_%d" % pp, [P, 1, LC], F32))
                tdt = mc(nc.sbuf_tensor("tdt_%d" % pp, [P, 1, LC], F32))

                with nc.Block() as blk:
                    bw = s_w.n

                    @blk.gpsimd
                    def _(gp):
                        gp.dma_start(
                            out=W1[:],
                            in_=wxT_h.rearrange("(kb p) m -> p kb m", p=P)
                        ).then_inc(s_w.h, 16)
                        gp.dma_start(
                            out=Wz_sb[:],
                            in_=wzT_h.rearrange("(kb p) m -> p kb m", p=P)
                        ).then_inc(s_w.h, 16)
                        gp.dma_start(
                            out=Wo_sb[:],
                            in_=woT_h.rearrange("(kb p) m -> p kb m", p=P)
                        ).then_inc(s_w.h, 16)
                    s_w.n = bw + 48

                bufs = (W1, Wz_sb, Wo_sb, xproj, dt_sb, u_sb, zs_sb, y_sb,
                        dtun_bf, B_bf, C_bf, Bb_sb, Cb_sb, a_bufs, am1_bufs,
                        w2_bufs, h_bufs, ch_bufs, tdt)
                for c in range(NCH):
                    with nc.named_scope("p%d_mc%d" % (pp, c)):
                        emit_mamba_chunk(pp, hh, c, bufs)

                if hh == 0:
                    with nc.Block() as blk:
                        @blk.vector
                        def _(v):
                            v.tensor_copy(pad_save[:], xproj[:, :, LC:LC + 3])
                            v.drain()

        def emit_mamba_chunk(pp, hh, c, bufs):
            (W1, Wz_sb, Wo_sb, xproj, dt_sb, u_sb, zs_sb, y_sb, dtun_bf,
             B_bf, C_bf, Bb_sb, Cb_sb, a_bufs, am1_bufs, w2_bufs, h_bufs,
             ch_bufs, tdt) = bufs
            t0 = c * LC

            with nc.named_scope("mA_%d_%d" % (pp, c)), \
                    nc.Block() as blk, contextlib.ExitStack() as sc:
                pc = sc.enter_context
                ps_m = pc(nc.psum_tensor("psA_%d_%d" % (pp, c), [P, 2, 512], F32))
                ps_p1 = pc(nc.psum_tensor("psP1_%d_%d" % (pp, c), [80, LC], F32))
                ps_p2 = pc(nc.psum_tensor("psP2_%d_%d" % (pp, c), [16, LC], F32))

                bw, bp2, bev, bd2, be2 = (s_w.n, s_p2.n, s_ev.n, s_d2.n,
                                          s_e2.n)

                @blk.tensor
                def _(t):
                    t.wait_ge(s_w.h, bw)
                    for db in range(NDB):
                        sl = db % 2
                        if db >= 2:
                            t.wait_ge(s_ev.h, bev + db - 1)
                        for k in range(NDB):
                            t.matmul(ps_m[:, sl, 0:LC],
                                     W1[:, k, db * P:(db + 1) * P],
                                     xn[:, k, t0:t0 + LC],
                                     start=(k == 0), stop=(k == NDB - 1)
                                     ).then_inc(s_p2.h,
                                                1 if k == NDB - 1 else 0)
                    for k in range(NDB):
                        t.matmul(ps_p1[:], wpT_sb[:, k, 0:80],
                                 xn[:, k, t0:t0 + LC],
                                 start=(k == 0), stop=(k == NDB - 1))
                    mm = None
                    for k in range(NDB):
                        mm = t.matmul(ps_p2[:], wpT_sb[:, k, 80:96],
                                      xn[:, k, t0:t0 + LC],
                                      start=(k == 0), stop=(k == NDB - 1))
                    mm.then_inc(s_p2.h, 1)
                    t.wait_ge(s_ev.h, bev + NDB + 1)
                    for db in range(NDB):
                        sl = db % 2
                        if db >= 2:
                            t.wait_ge(s_e2.h, be2 + db - 1)
                        t.matmul(ps_m[:, sl, 0:LC],
                                 wdtT_sb[:, db * P:(db + 1) * P],
                                 dtun_bf[:], start=True, stop=True
                                 ).then_inc(s_p2.h, 1)
                    for db in range(NDB):
                        sl = db % 2
                        if db < 2:
                            t.wait_ge(s_e2.h, be2 + 7 + db)
                        else:
                            t.wait_ge(s_e2.h, be2 + NDB + db - 1)
                        for k in range(NDB):
                            t.matmul(ps_m[:, sl, 0:LC],
                                     Wz_sb[:, k, db * P:(db + 1) * P],
                                     xn[:, k, t0:t0 + LC],
                                     start=(k == 0), stop=(k == NDB - 1)
                                     ).then_inc(s_p2.h,
                                                1 if k == NDB - 1 else 0)
                s_p2.n = bp2 + NDB + 1 + NDB + NDB

                @blk.scalar
                def _(s):
                    s.wait_ge(s_d2.h, bd2 + 1)
                    for db in range(NDB):
                        sl = db % 2
                        s.wait_ge(s_p2.h, bp2 + db + 1)
                        s.activation(xproj[:, db, 3:3 + LC], ps_m[:, sl, 0:LC],
                                     AF.Identity).then_inc(s_ev.h, 1)
                    s.wait_ge(s_p2.h, bp2 + NDB + 1)
                    s.activation(dtun_bf[:], ps_p1[0:64, :], AF.Identity)
                    s.activation(B_bf[:], ps_p1[64:80, :], AF.Identity)
                    s.activation(C_bf[:], ps_p2[:], AF.Identity)
                    s.drain().then_inc(s_ev.h, 1)
                    for db in range(NDB):
                        sl = db % 2
                        lnb = (ch_bufs[:, 0, :] if db % 2 == 0
                               else w2_bufs[:, 0, :])
                        s.wait_ge(s_p2.h, bp2 + NDB + 1 + db + 1)
                        if db >= 2:
                            s.wait_ge(s_d2.h, bd2 + db)
                        s.activation(tdt[:, 0, :], ps_m[:, sl, 0:LC], AF.Identity,
                                     bias=bdt_sb[:, db:db + 1])
                        if fence:
                            s.drain()
                        s.activation(lnb, tdt[:, 0, :], AF.Abs)
                        if fence:
                            s.drain()
                        s.activation(lnb, lnb, AF.Exp, scale=-1.0)
                        if fence:
                            s.drain()
                        s.activation(lnb, lnb, AF.Ln, bias=1.0)
                        if fence:
                            s.drain()
                        s.activation(dt_sb[:, db, :], tdt[:, 0, :],
                                     AF.Relu).then_inc(s_e2.h, 1)
                    for db in range(NDB):
                        sl = db % 2
                        s.wait_ge(s_p2.h, bp2 + 2 * NDB + 1 + db + 1)
                        s.activation(zs_sb[:, db, :], ps_m[:, sl, 0:LC],
                                     AF.Silu).then_inc(s_e2.h, 1)
                    s.wait_ge(s_d2.h, bd2 + NDB + 2)
                    op = None
                    for db in range(NDB):
                        if fence:
                            s.drain()
                        op = s.activation(u_sb[:, db, :], u_sb[:, db, :],
                                          AF.Silu)
                    op.then_inc(s_e2.h, 1)
                s_ev.n = bev + NDB + 1
                s_e2.n = be2 + NDB + NDB + 1

                @blk.vector
                def _(v):
                    if c == 0 and hh == 0:
                        v.memset(xproj[:, :, 0:3], 0.0)
                    elif c == 0:
                        v.tensor_copy(xproj[:, :, 0:3], pad_save[:])
                    else:
                        v.tensor_copy(xproj[:, :, 0:3], xproj[:, :, LC:LC + 3])
                    v.drain().then_inc(s_d2.h, 1)
                    for db in range(NDB):
                        lnb = (ch_bufs[:, 0, :] if db % 2 == 0
                               else w2_bufs[:, 0, :])
                        v.wait_ge(s_e2.h, be2 + db + 1)
                        v.tensor_tensor(dt_sb[:, db, :], dt_sb[:, db, :],
                                        lnb, ALU.add).then_inc(s_d2.h, 1)
                    v.wait_ge(s_ev.h, bev + NDB)
                    op = None
                    for db in range(NDB):
                        v.tensor_scalar(u_sb[:, db, :], xproj[:, db, 0:LC],
                                        convw_sb[:, db, 0:1], None, ALU.mult)
                        for k in range(1, 4):
                            if fence:
                                v.drain()
                            op = v.scalar_tensor_tensor(
                                u_sb[:, db, :], xproj[:, db, k:k + LC],
                                convw_sb[:, db, k:k + 1], u_sb[:, db, :],
                                ALU.mult, ALU.add)
                    op.then_inc(s_d2.h, 1)
                s_d2.n = bd2 + 2 + NDB

            if tap and tap.startswith("m_") and pp == 0 and c == 0:
                with nc.Block() as blk:
                    @blk.sync
                    def _(sync):
                        n16 = 0
                        if tap == "m_xpj":
                            sync.dma_start(out=tap_h[:, :, 0:LC + 3],
                                           in_=xproj[:]).then_inc(s_x.h, 16)
                            n16 = 16
                        elif tap == "m_dt":
                            sync.dma_start(out=tap_h[:, :, 0:LC],
                                           in_=dt_sb[:]).then_inc(s_x.h, 16)
                            n16 = 16
                        elif tap == "m_u":
                            sync.dma_start(out=tap_h[:, :, 0:LC],
                                           in_=u_sb[:]).then_inc(s_x.h, 16)
                            n16 = 16
                        elif tap == "m_zs":
                            sync.dma_start(out=tap_h[:, :, 0:LC],
                                           in_=zs_sb[:]).then_inc(s_x.h, 16)
                            n16 = 16
                        sync.wait_ge(s_x.h, s_x.n + n16)
                    if tap in ("m_xpj", "m_dt", "m_u", "m_zs"):
                        s_x.n += 16

                    @blk.gpsimd
                    def _(gp):
                        if tap == "m_BC":
                            gp.dma_start(out=tap_h[0:16, 0, 0:LC],
                                         in_=B_bf[:]).then_inc(s_w.h, 16)
                            gp.dma_start(out=tap_h[0:16, 1, 0:LC],
                                         in_=C_bf[:]).then_inc(s_w.h, 16)
                            gp.wait_ge(s_w.h, s_w.n + 32)
                    if tap == "m_BC":
                        s_w.n += 32

            with nc.named_scope("mB_%d_%d" % (pp, c)), \
                    nc.Block() as blk, contextlib.ExitStack() as sc:
                pc = sc.enter_context
                ps_bc = pc(nc.psum_tensor("psB_%d_%d" % (pp, c), [P, 4, 512], F32))

                bp2, bev, bd2, be2, bpl = (s_p2.n, s_ev.n, s_d2.n, s_e2.n,
                                           s_pl.n)

                @blk.tensor
                def _(t):
                    for n in range(NST):
                        nsl = n % 2
                        if n >= 2:
                            t.wait_ge(s_e2.h, be2 + n - 1)
                        t.matmul(ps_bc[:, nsl * 2, 0:LC], pick16[:, n, :],
                                 B_bf[:], start=True, stop=True)
                        t.matmul(ps_bc[:, nsl * 2 + 1, 0:LC], pick16[:, n, :],
                                 C_bf[:], start=True, stop=True
                                 ).then_inc(s_p2.h, 1)
                s_p2.n = bp2 + NST

                @blk.scalar
                def _(s):
                    for n in range(NST):
                        nsl = n % 2
                        s.wait_ge(s_p2.h, bp2 + n + 1)
                        if n >= 2:
                            s.wait_ge(s_pl.h, bpl + (n - 2) * NDB + NDB)
                        s.activation(Bb_sb[:, nsl, :], ps_bc[:, nsl * 2, 0:LC],
                                     AF.Identity)
                        s.activation(Cb_sb[:, nsl, :],
                                     ps_bc[:, nsl * 2 + 1, 0:LC],
                                     AF.Identity).then_inc(s_e2.h, 1)
                        for db in range(NDB):
                            ii = n * NDB + db
                            sl = ii % 2
                            if ii >= 2:
                                s.wait_ge(s_d2.h, bd2 + (ii - 2) * 3 + 3)
                            s.activation(a_bufs[:, sl, :], dt_sb[:, db, :],
                                         AF.Exp, scale=A_sb[:, db, n:n + 1])
                            if fence:
                                s.drain()
                            s.activation(am1_bufs[:, sl, :], a_bufs[:, sl, :],
                                         AF.Identity,
                                         scale=invA_sb[:, db, n:n + 1],
                                         bias=ninvA_sb[:, db, n:n + 1]
                                         ).then_inc(s_ev.h, 1)
                s_ev.n = bev + NST * NDB
                s_e2.n = be2 + NST

                @blk.vector
                def _(v):
                    for n in range(NST):
                        nsl = n % 2
                        v.wait_ge(s_e2.h, be2 + n + 1)
                        for db in range(NDB):
                            ii = n * NDB + db
                            sl = ii % 2
                            v.wait_ge(s_ev.h, bev + ii + 1)
                            if ii >= 2:
                                v.wait_ge(s_pl.h, bpl + ii - 1)
                            v.tensor_tensor(w2_bufs[:, sl, :],
                                            u_sb[:, db, :],
                                            Bb_sb[:, nsl, :], ALU.mult)
                            if fence:
                                v.drain()
                            v.tensor_tensor(w2_bufs[:, sl, :],
                                            am1_bufs[:, sl, :],
                                            w2_bufs[:, sl, :], ALU.mult)
                            if fence:
                                v.drain()
                            v.tensor_tensor_scan(h_bufs[:, sl, :],
                                                 a_bufs[:, sl, :],
                                                 w2_bufs[:, sl, :],
                                                 h_state[:, db, n:n + 1],
                                                 ALU.mult, ALU.add
                                                 ).then_inc(s_d2.h, 3)
                s_d2.n = bd2 + NST * NDB * 3

                @blk.gpsimd
                def _(gp):
                    for n in range(NST):
                        nsl = n % 2
                        for db in range(NDB):
                            ii = n * NDB + db
                            sl = ii % 2
                            gp.wait_ge(s_d2.h, bd2 + ii * 3 + 3)
                            if fence:
                                gp.drain()
                            gp.tensor_tensor(ch_bufs[:, 0, :],
                                             h_bufs[:, sl, :],
                                             Cb_sb[:, nsl, :], ALU.mult)
                            if fence:
                                gp.drain()
                            gp.tensor_copy(h_state[:, db, n:n + 1],
                                           h_bufs[:, sl, LC - 1:LC])
                            if n == 0:
                                gp.tensor_copy(y_sb[:, db, :],
                                               ch_bufs[:, 0, :]
                                               ).then_inc(s_pl.h, 1)
                            else:
                                gp.tensor_tensor(y_sb[:, db, :],
                                                 y_sb[:, db, :],
                                                 ch_bufs[:, 0, :], ALU.add
                                                 ).then_inc(s_pl.h, 1)
                    gp.drain()
                s_pl.n = bpl + NST * NDB

            if tap == "m_y" and pp == 0 and c == 0:
                with nc.Block() as blk:
                    @blk.sync
                    def _(sync):
                        sync.dma_start(out=tap_h[:, :, 0:LC],
                                       in_=y_sb[:]).then_inc(s_x.h, 16)
                        sync.wait_ge(s_x.h, s_x.n + 16)
                    s_x.n += 16

            with nc.named_scope("mC_%d_%d" % (pp, c)), \
                    nc.Block() as blk, contextlib.ExitStack() as sc:
                pc = sc.enter_context
                ps_wo = pc(nc.psum_tensor("psC_%d_%d" % (pp, c), [P, 2, 512], F32))

                bw, bp2, bd2 = s_w.n, s_p2.n, s_d2.n

                @blk.vector
                def _(v):
                    op = None
                    for db in range(NDB):
                        if fence:
                            v.drain()
                        v.scalar_tensor_tensor(ch_bufs[:, 0, :], u_sb[:, db, :],
                                               dp_sb[:, db:db + 1],
                                               y_sb[:, db, :], ALU.mult,
                                               ALU.add)
                        if fence:
                            v.drain()
                        op = v.tensor_tensor(xn[:, db, t0:t0 + LC],
                                             ch_bufs[:, 0, :],
                                             zs_sb[:, db, :], ALU.mult)
                    op.then_inc(s_d2.h, 1)
                    for db in range(NDB):
                        sl = db % 2
                        v.wait_ge(s_p2.h, bp2 + db + 1)
                        v.tensor_tensor(xr[:, db, t0:t0 + LC],
                                        xr[:, db, t0:t0 + LC],
                                        ps_wo[:, sl, 0:LC], ALU.add
                                        ).then_inc(s_d2.h, 1)
                s_d2.n = bd2 + 1 + NDB

                @blk.tensor
                def _(t):
                    t.wait_ge(s_d2.h, bd2 + 1)
                    for db in range(NDB):
                        sl = db % 2
                        if db >= 2:
                            t.wait_ge(s_d2.h, bd2 + 1 + db - 1)
                        mm = None
                        for k in range(NDB):
                            mm = t.matmul(ps_wo[:, sl, 0:LC],
                                          Wo_sb[:, k, db * P:(db + 1) * P],
                                          xn[:, k, t0:t0 + LC],
                                          start=(k == 0), stop=(k == NDB - 1))
                        mm.then_inc(s_p2.h, 1)
                s_p2.n = bp2 + NDB


        def emit_store(b, hh):
            with nc.Block() as blk:
                bx = s_x.n

                @blk.sync
                def _(sync):
                    dst = out_h[b].rearrange("(db p) l -> p db l", p=P)[
                        :, :, hh * LP:(hh + 1) * LP]
                    sync.dma_start(out=dst, in_=xr[:]).then_inc(s_x.h, 16)
                    sync.wait_ge(s_x.h, bx + 16)
                s_x.n = bx + 16

        for pp in range(npass):
            b, hh = divmod(pp, NH)
            if hh == 0 and do_mamba:
                with nc.Block() as blk:
                    @blk.vector
                    def _(v):
                        v.memset(h_state[:], 0.0)
                        v.drain()

            def ffn_call(tag, xin_dma=None, first_call=False, split=False):
                with contextlib.ExitStack() as fs:
                    fc = fs.enter_context
                    sg = fc(nc.sbuf_tensor("sgw_%s" % tag, [P, NDB, 1408], BF16))
                    su = fc(nc.sbuf_tensor("suw_%s" % tag, [P, NDB, 1408], BF16))
                    sd = fc(nc.sbuf_tensor("sdw_%s" % tag, [P, NJ, DM], BF16))
                    inter = fc(nc.sbuf_tensor("int_%s" % tag, [P, NJ, LP], BF16))
                    wbs = fc(nc.sbuf_tensor("wbs_%s" % tag, [P, 7, LP], F32))

                    def prefetch(fc0):
                        with nc.Block() as pblk:
                            def _pf(q):
                                pg, pu, pd = ((gT_h, uT_h, dT_h) if fc0
                                              else (scrg_h, scru_h, scrd_h))
                                gsl = pg[0].rearrange(
                                    "(db p) i -> p db i", p=P)[:, :, 0:1408]
                                usl = pu[0].rearrange(
                                    "(db p) i -> p db i", p=P)[:, :, 0:1408]
                                dsl = pd[0].rearrange(
                                    "(jj p) dd -> p jj dd", p=P)[:, 0:NJ, :]
                                q.dma_start(out=sg[:], in_=gsl
                                            ).then_inc(s_w.h, 16)
                                q.dma_start(out=su[:], in_=usl
                                            ).then_inc(s_w.h, 16)
                                q.dma_start(out=sd[:], in_=dsl
                                            ).then_inc(s_w.h, 16)
                            if fc0:
                                @pblk.gpsimd
                                def _(gp):
                                    _pf(gp)
                            else:
                                @pblk.sync
                                def _(sync):
                                    _pf(sync)
                            s_w.n += 48

                    if split:
                        for hw in range(2):
                            fc0 = first_call and hw == 0
                            prefetch(fc0)
                            emit_ln("%s_h%d" % (tag, hw),
                                    xin_dma=(xin_dma if hw == 0 else None),
                                    tok0=hw * 512, ntok=512)
                            emit_router("%s_h%d" % (tag, hw), wbs,
                                        tok0=hw * 512, ntok=512)
                            emit_ffn("%s_h%d" % (tag, hw), wbs, sg, su, sd,
                                     inter, skip_first_load=True,
                                     first_call=fc0, tok0=hw * 512, ntok=512)
                        return
                    prefetch(first_call)
                    emit_ln(tag, xin_dma=xin_dma)
                    emit_router(tag, wbs)
                    if tap == "wbs" and tag.endswith("_0") and pp == 0:
                        with nc.Block() as blk2:
                            @blk2.sync
                            def _(sync):
                                for e in range(7):
                                    sync.dma_start(out=tap_h[:, e, :],
                                                   in_=wbs[:, e, :]
                                                   ).then_inc(s_x.h, 16)
                                sync.wait_ge(s_x.h, s_x.n + 112)
                            s_x.n += 112
                    emit_ffn(tag, wbs, sg, su, sd, inter,
                             skip_first_load=True, first_call=first_call)

            if not do_ffn1:
                with nc.named_scope("p%d_ln1" % pp):
                    emit_ln("%d_0" % pp, xin_dma=(b, hh))
            if tap and pp == 0:
                with nc.Block() as blk, \
                        nc.sbuf_tensor("tapf", [P, NDB, LP], F32) as tapf:
                    @blk.vector
                    def _(v):
                        for db in range(NDB):
                            v.tensor_copy(tapf[:, db, :], xn[:, db, :])
                        v.drain().then_inc(s_d2.h, 1)
                    s_d2.n += 1

                    @blk.sync
                    def _(sync):
                        sync.wait_ge(s_d2.h, s_d2.n)
                        sync.dma_start(out=tap_h, in_=tapf[:]
                                       ).then_inc(s_x.h, 16)
                        sync.wait_ge(s_x.h, s_x.n + 16)
                    s_x.n += 16
            if do_ffn1:
                with nc.named_scope("p%d_ffn1" % pp):
                    ffn_call("%d_0" % pp, xin_dma=(b, hh),
                             first_call=(pp == 0))
            if do_mamba:
                with nc.named_scope("p%d_ln2" % pp):
                    emit_ln("%d_1" % pp)
                with nc.named_scope("p%d_mamba" % pp):
                    emit_mamba(pp, hh)
            if do_ffn2:
                with nc.named_scope("p%d_ffn2" % pp):
                    ffn_call("%d_2" % pp, split=split2)
            with nc.named_scope("p%d_store" % pp):
                emit_store(b, hh)

    return nc


def _host_layout(inputs):
    f32 = np.float32
    x = np.ascontiguousarray(
        np.asarray(inputs["L0_attn_x"], dtype=f32).transpose(0, 2, 1))
    wxT = np.ascontiguousarray(np.asarray(inputs["L0_attn_Wx"], f32).T)
    wzT = np.ascontiguousarray(np.asarray(inputs["L0_attn_Wz"], f32).T)
    woT = np.ascontiguousarray(np.asarray(inputs["L0_attn_Wo"], f32).T)
    wpT = np.ascontiguousarray(np.asarray(inputs["L0_attn_Wp"], f32).T)
    wdtT = np.ascontiguousarray(np.asarray(inputs["L0_attn_Wdt"], f32).T)
    bdt = np.ascontiguousarray(
        np.asarray(inputs["L0_attn_bdt"], f32).reshape(NDB, P).T)
    alog = np.ascontiguousarray(
        np.asarray(inputs["L0_attn_A_log"], f32).reshape(NDB, P, NST)
        .transpose(1, 0, 2))
    dp = np.ascontiguousarray(
        np.asarray(inputs["L0_attn_Dp"], f32).reshape(NDB, P).T)
    convw = np.ascontiguousarray(
        np.asarray(inputs["L0_attn_conv_w"], f32)[:, 0, :]
        .reshape(NDB, P, 4).transpose(1, 0, 2))
    sgT = np.asarray(inputs["L0_ffn_sg"], f32).T
    suT = np.asarray(inputs["L0_ffn_su"], f32).T
    sdT = np.asarray(inputs["L0_ffn_sd"], f32).T
    rgT = np.asarray(inputs["L0_ffn_rg"], f32).transpose(0, 2, 1)
    ruT = np.asarray(inputs["L0_ffn_ru"], f32).transpose(0, 2, 1)
    rdT = np.asarray(inputs["L0_ffn_rd"], f32).transpose(0, 2, 1)
    gT = np.ascontiguousarray(np.concatenate([sgT[None], rgT], axis=0))
    uT = np.ascontiguousarray(np.concatenate([suT[None], ruT], axis=0))
    dT = np.ascontiguousarray(np.concatenate([sdT[None], rdT], axis=0))
    wrT = np.ascontiguousarray(np.asarray(inputs["L0_ffn_wr"], f32).T)
    rb = np.ascontiguousarray(
        np.asarray(inputs["L0_ffn_rb"], f32).reshape(7, 1))
    shared = dict(wxT=wxT, wzT=wzT, woT=woT, wpT=wpT, wdtT=wdtT, bdt=bdt,
                  alog=alog, dp=dp, convw=convw, gT=gT, uT=uT, dT=dT,
                  wrT=wrT, rb=rb)
    return x, shared


def kernel(**inputs):
    x, shared = _host_layout(inputs)
    nc = build()
    in_maps = []
    for c in range(8):
        m = dict(shared)
        m["x"] = np.ascontiguousarray(x[2 * c:2 * c + 2])
        in_maps.append(m)
    res = run_bass_kernel_spmd(nc, in_maps, core_ids=list(range(8)))
    outs = [res.results[c]["out"] for c in range(8)]
    full = np.concatenate(outs, axis=0)
    return np.ascontiguousarray(full.transpose(0, 2, 1)).astype(np.float32)
